# version 20
# speedup vs baseline: 1.1517x; 1.0054x over previous
import numpy as np
import concourse.bass as bass
import concourse.mybir as mybir
from concourse.bass_utils import run_bass_kernel_spmd
from contextlib import ExitStack

F32 = mybir.dt.float32
BF16 = mybir.dt.bfloat16
U8 = mybir.dt.uint8
AF = mybir.ActivationFunctionType
ALU = mybir.AluOpType

HID = 2048
T = 1024
NF = HID // 128
TC = 512
NTC = T // TC
H = 16
HKV = 4
DK = 128
QKVD = 3072
NQT = QKVD // 128
FFD = 4096
NFF = FFD // 128
C = 128
NCH = T // C
EPS_LN = 1e-5
EPS_L2 = 1e-6
EPS_RMS = 1e-5

_PHASES = 9
_DBG_QKVC = False
_DBG_STATE = False


class Sems:

    def __init__(self, stack, nc, names):
        self.h = {n: stack.enter_context(nc.semaphore(n)) for n in names}
        self.c = {n: 0 for n in names}

    def inc(self, n, k=1):
        self.c[n] += k
        return self.h[n], k

    def val(self, n):
        return self.c[n]

    def __getitem__(self, n):
        return self.h[n]


def build():
    nc = bass.Bass("TRN2", target_bir_lowering=False)

    def I(name, shape, dtype=F32):
        return nc.dram_tensor(name, shape, dtype, kind="ExternalInput").ap()

    x_in = I("x", [HID, T])
    qkvT = I("qkvT", [NQT, 128, 16 * 128])
    gT = I("gT", [NF, 128, 16 * 128])
    bTg = I("bT", [HID, H])
    gkTg = I("gkT", [HID, H])
    oTw = I("oT", [NF, 128, 16 * 128])
    WgT = I("WgT", [HID, 2])
    W0aT = I("W0aT", [NFF // 2, 128, 16 * 256])
    W0bT = I("W0bT", [NF, 128, 32 * 128])
    W1aT = I("W1aT", [NFF // 2, 128, 16 * 256])
    W1bT = I("W1bT", [NF, 128, 32 * 128])
    convw = I("convw", [QKVD, 4])
    b0a_c = I("b0a_c", [128, NFF])
    b1a_c = I("b1a_c", [128, NFF])
    b0b_r = I("b0b_r", [1, HID])
    b1b_r = I("b1b_r", [1, HID])
    onw_b = I("onw_b", [128, 128])
    halfm = I("halfm", [128, 1])

    out_d = nc.dram_tensor("out", [HID, T], F32, kind="ExternalOutput").ap()

    W0ab = nc.dram_tensor("W0ab", [NFF // 2, 128, 16 * 256], BF16)
    W0bb = nc.dram_tensor("W0bb", [NF, 128, 32 * 128], BF16)
    W1ab = nc.dram_tensor("W1ab", [NFF // 2, 128, 16 * 256], BF16)
    W1bb = nc.dram_tensor("W1bb", [NF, 128, 32 * 128], BF16)
    qkvb = nc.dram_tensor("qkvb", [NQT, 128, 16 * 128], BF16)
    cc1_in = nc.dram_tensor("cc1_in", [128, NQT * 3], F32)
    cc1_out = nc.dram_tensor("cc1_out", [256, NQT * 3], F32)
    cc2_in = nc.dram_tensor("cc2_in", [128, H * DK], F32)
    cc2_out = nc.dram_tensor("cc2_out", [256, H * DK], F32)
    RG = [[0, 1], [2, 3], [4, 5], [6, 7]]

    with ExitStack() as stk:
        S = stk.enter_context

        A1C = 37 * 1024
        A2C = 24 * 1024
        resid = S(nc.sbuf_tensor("resid", [128, NF * T], F32))
        arena1 = S(nc.sbuf_tensor("arena1", [128, A1C], BF16))
        arena2 = S(nc.sbuf_tensor("arena2", [128, A2C], BF16))
        identb = S(nc.sbuf_tensor("identb", [128, 128], BF16))
        identf = S(nc.sbuf_tensor("identf", [128, 128], F32))
        ones1b = S(nc.sbuf_tensor("ones1b", [1, 128], BF16))
        onescolb = S(nc.sbuf_tensor("onescolb", [128, 1], BF16))
        trilS = S(nc.sbuf_tensor("trilS", [128, 128], F32))
        triuI4 = S(nc.sbuf_tensor("triuI4", [128, 512], BF16))
        cst = S(nc.sbuf_tensor("cst", [128, 12], F32))
        convw_s = S(nc.sbuf_tensor("convw_s", [128, NQT * 4], F32))
        b0a_s = S(nc.sbuf_tensor("b0a_s", [128, NFF], F32))
        b1a_s = S(nc.sbuf_tensor("b1a_s", [128, NFF], F32))
        onesrow = S(nc.sbuf_tensor("onesrow", [1, TC], BF16))
        onw_s = S(nc.sbuf_tensor("onw_s", [128, 128], F32))
        halfm_s = S(nc.sbuf_tensor("halfm_s", [128, 1], F32))
        wgT_s = S(nc.sbuf_tensor("wgT_s", [128, NF * 2], BF16))
        wgdiff = S(nc.sbuf_tensor("wgdiff", [128, NF], BF16))
        bT_s = S(nc.sbuf_tensor("bT_s", [128, NF * H], BF16))
        gkT_s = S(nc.sbuf_tensor("gkT_s", [128, NF * H], BF16))
        rowA = S(nc.sbuf_tensor("rowA", [1, TC], F32))
        rowB = S(nc.sbuf_tensor("rowB", [1, TC], F32))
        rowC = S(nc.sbuf_tensor("rowC", [1, TC], BF16))
        rowD = S(nc.sbuf_tensor("rowD", [1, TC], BF16))
        spfm = S(nc.sbuf_tensor("spfm", [16, T], BF16))
        befm = S(nc.sbuf_tensor("befm", [16, T], BF16))
        a_col = S(nc.sbuf_tensor("a_col", [128, NCH * 16], F32))
        r_col = S(nc.sbuf_tensor("r_col", [128, NCH * 16], F32))
        be_col = S(nc.sbuf_tensor("be_col", [128, NCH * 16], F32))
        aC_col = S(nc.sbuf_tensor("aC_col", [128, NCH * 16], F32))
        rms_s = S(nc.sbuf_tensor("rms_s", [128, 16], F32))
        scr16 = S(nc.sbuf_tensor("scr16", [16, TC], F32))
        triuN16 = S(nc.sbuf_tensor("triuN16", [128, 128], BF16))
        stage3 = S(nc.sbuf_tensor("stage3", [128, NQT * 3], F32))

        PS = [S(nc.psum_tensor(f"ps{i}", [128, 512], F32)) for i in range(8)]

        sems = Sems(stk, nc, ["w", "xd", "mm", "ev", "ve", "cc", "st", "gc", "wc", "mmB", "evB", "veB"])

        def A1(off_kb2, ncols, dt=None):
            ap = arena1[:, off_kb2:off_kb2 + ncols]
            return ap if dt is None else ap.bitcast(dt)

        def A2(off, ncols, dt=None):
            ap = arena2[:, off:off + ncols]
            return ap if dt is None else ap.bitcast(dt)

        wstg = [None, None]

        with nc.Block(no_gpsimd_drain=True) as blk:
            @blk.sync
            def _(e):
                e.dma_start(out=resid[:, :].rearrange("p (f t) -> p f t", f=NF),
                            in_=x_in.rearrange("(f p) t -> p f t", p=128)).then_inc(*sems.inc("xd", 16))
                e.dma_start(out=b0a_s[:, :], in_=b0a_c[:, :]).then_inc(*sems.inc("xd", 16))
                e.dma_start(out=b1a_s[:, :], in_=b1a_c[:, :]).then_inc(*sems.inc("xd", 16))

                e.dma_start(out=onw_s[:, :], in_=onw_b[:, :]).then_inc(*sems.inc("xd", 16))
                e.dma_start(out=halfm_s[:, :], in_=halfm[:, :]).then_inc(*sems.inc("xd", 16))
                e.dma_start(out=convw_s[:, :].rearrange("p (t c) -> p t c", c=4),
                            in_=convw.rearrange("(t p) c -> p t c", p=128)).then_inc(*sems.inc("xd", 16))

            @blk.gpsimd
            def _(g):
                g.dma_start(out=wgT_s[:, :].rearrange("p (k c) -> p k c", c=2),
                            in_=WgT.rearrange("(k p) c -> p k c", p=128)).then_inc(*sems.inc("w", 16))
                g.dma_start(out=bT_s[:, :].rearrange("p (k c) -> p k c", c=H),
                            in_=bTg.rearrange("(k p) c -> p k c", p=128)).then_inc(*sems.inc("w", 16))
                g.dma_start(out=gkT_s[:, :].rearrange("p (k c) -> p k c", c=H),
                            in_=gkTg.rearrange("(k p) c -> p k c", p=128)).then_inc(*sems.inc("w", 16))
                g.iota(identf[:, :], [[-1, 128]], base=0, channel_multiplier=1,
                       allow_small_or_imprecise_dtypes=True)
                g.tensor_scalar(trilS[:, :], identf[:, :], 0.0, None, ALU.is_gt)
                g.iota(triuI4[:, :], [[0, 4], [-1, 128]], base=0, channel_multiplier=1,
                       allow_small_or_imprecise_dtypes=True)
                g.tensor_scalar(triuI4[:, :], triuI4[:, :], 0.0, None, ALU.is_le)
                g.tensor_scalar(identb[:, :], identf[:, :], 0.0, None, ALU.is_equal)
                g.tensor_scalar(identf[:, :], identf[:, :], 0.0, None, ALU.is_equal)
                g.tensor_scalar(triuN16[:, :], triuI4[:, 0:128], -1.0 / 16.0, None,
                                ALU.mult)
                g.memset(ones1b[:, :], 1.0)
                g.memset(onescolb[:, :], 1.0)
                for i, v in enumerate(CONSTS):
                    g.memset(cst[:, i:i + 1], v)
                g.memset(onesrow[:, :], 1.0)
                g.wait_ge(sems["xd"], sems.val("xd"))
                g.wait_ge(sems["w"], sems.val("w"))
                wg3 = wgT_s[:, :].rearrange("p (k c) -> p k c", c=2)
                g.tensor_tensor(wgdiff[:, :].rearrange("p (k o) -> p k o", o=1),
                                wg3[:, :, 0:1], wg3[:, :, 1:2], ALU.subtract)

        for i, v in enumerate(CONSTS):
            if v not in (0.0, 1.0):
                nc.const_aps.register(F32, v, cst[:, i:i + 1])
        nc.const_aps.register(F32, 1.0, cst[:, 0:1])

        def emit_ln_chunk(tc0, xn_ap):
            return emit_ln_chunk_to(tc0, xn_ap, TC, dst0=0)

        def emit_ln_chunk_to(tc0, xn_ap, fstride, dst0=None):
            if dst0 is None:
                dst0 = tc0
            XSL = [xn_ap[:, f * fstride + dst0: f * fstride + dst0 + TC]
                   for f in range(NF)]
            with nc.Block(no_gpsimd_drain=True) as b:
                @b.scalar
                def _(e):
                    for f in range(NF):
                        ins = e.activation(XSL[f],
                                           resid[:, f * T + tc0:f * T + tc0 + TC], AF.Copy)
                    ins.then_inc(*sems.inc("ev"))

                @b.tensor
                def _(t):
                    t.wait_ge(sems["ev"], sems.val("ev"))
                    for f in range(NF):
                        ins = t.matmul(PS[4][0:1, 0:TC], onescolb[:, :],
                                       XSL[f],
                                       start=(f == 0), stop=(f == NF - 1))
                    ins.then_inc(*sems.inc("mm"))

                @b.vector
                def _(v):
                    v.wait_ge(sems["mm"], sems.val("mm"))
                    for f in range(NF):
                        ins = v.tensor_tensor(XSL[f], XSL[f], XSL[f], ALU.mult)
                    ins.then_inc(*sems.inc("ve"))

                @b.tensor
                def _(t):
                    t.wait_ge(sems["ve"], sems.val("ve"))
                    for f in range(NF):
                        ins = t.matmul(PS[5][0:1, 0:TC], onescolb[:, :],
                                       XSL[f],
                                       start=(f == 0), stop=(f == NF - 1))
                    ins.then_inc(*sems.inc("mm"))

                @b.vector
                def _(v):
                    v.wait_ge(sems["mm"], sems.val("mm"))
                    v.tensor_scalar(rowA[:, :], PS[4][0:1, 0:TC], 1.0 / HID, None, ALU.mult)
                    v.tensor_tensor(rowB[:, :], rowA[:, :], rowA[:, :], ALU.mult)
                    v.scalar_tensor_tensor(rowB[:, :], PS[5][0:1, 0:TC], 1.0 / HID,
                                           rowB[:, :], ALU.mult, ALU.subtract)
                    v.tensor_copy(rowC[:, :], rowA[:, :]).then_inc(*sems.inc("ve"))

                @b.vector
                def _(v):
                    v.wait_ge(sems["ve"], sems.val("ve"))
                    v.tensor_scalar(rowB[:, :], rowB[:, :], EPS_LN, None, ALU.add)
                    v.reciprocal(rowB[:, :], rowB[:, :]).then_inc(*sems.inc("ve"))

                @b.scalar
                def _(e):
                    e.wait_ge(sems["ve"], sems.val("ve"))
                    e.activation(rowD[:, :], rowB[:, :], AF.Sqrt).then_inc(*sems.inc("ev"))

                @b.tensor
                def _(t):
                    t.wait_ge(sems["ev"], sems.val("ev"))
                    t.matmul(PS[6][:, 0:TC], ones1b[:, :], rowC[:, :], start=True, stop=True)
                    t.matmul(PS[7][:, 0:TC], ones1b[:, :], rowD[:, :],
                             start=True, stop=True).then_inc(*sems.inc("mm"))

                @b.vector
                def _(v):
                    v.wait_ge(sems["mm"], sems.val("mm"))
                    for f in range(NF):
                        sl = XSL[f]
                        v.tensor_tensor(sl, resid[:, f * T + tc0:f * T + tc0 + TC],
                                        PS[6][:, 0:TC], ALU.subtract)
                        ins = v.tensor_tensor(sl, sl, PS[7][:, 0:TC], ALU.mult)
                    ins.then_inc(*sems.inc("ve"))

        def stream_matmul(wT_ap, kdim, mtiles, rhs_ap, ncols, wsl, mg, nsplit,
                          evac_fn, bias_row=None, rstride=None, roffs=(0,),
                          evac_engine="scalar", wmode="cast", wstore=None,
                          carry_in=False, prefetch_ap=None, rhs_evs=None,
                          blk=None, bank_base=0, entry_ve=None, evac_guard_mm=None):
            ktiles = kdim // 128
            sc = mg * 128
            nst = mtiles // mg
            nr = len(roffs)
            w0 = sems.val("w")
            mm0 = sems.val("mm")
            ev0 = sems.val("ev")
            wcq0 = sems.val("wc")
            nq = (wsl[0].shape[1] // 1024) if wmode == "castv" else 0
            w_after = [w0 + (s + 1) * (nq if nq else 1) * 16 for s in range(nst)]
            if carry_in:
                w_after = [w0 + max(0, s - 1) * 16 for s in range(nst)]
            mm_after = [mm0 + (s + 1) for s in range(nst)]
            ev_after = [ev0 + (s + 1) for s in range(nst)]
            sems.c["w"] = w_after[-1] if not carry_in else w0 + (nst - 2) * 16
            sems.c["mm"] = mm_after[-1]
            sems.c["ev"] = ev_after[-1]
            if wmode == "castv":
                sems.c["wc"] = wcq0 + nst * nq
            ret_ev = list(ev_after)
            rstride_ = ncols if rstride is None else rstride
            wb0 = sems.val("st")
            if wstore is not None:
                sems.c["st"] = wb0 + nst * 16

            from contextlib import nullcontext
            with (nc.Block(no_gpsimd_drain=True) if blk is None else nullcontext(blk)) as b:
                def _load_body(g):
                    for s2 in range(nst):
                        if carry_in and s2 < 2:
                            continue
                        if s2 >= 2:
                            g.wait_ge(sems["mm"], mm_after[s2 - 2])
                            if wstore is not None:
                                g.wait_ge(sems["st"], wb0 + (s2 - 1) * 16)
                        slot = wsl[s2 % 2]
                        g.dma_start(out=slot[:, :],
                                    in_=wT_ap[s2]).then_inc(sems["w"], 16)
                    if prefetch_ap is not None:
                        if wstore is not None:
                            g.wait_ge(sems["st"], sems.val("st"))
                        g.wait_ge(sems["mm"], mm_after[nst - 2])
                        g.dma_start(out=wsl[0][:, :],
                                    in_=prefetch_ap[0]).then_inc(sems["w"], 16)
                        g.wait_ge(sems["mm"], mm_after[nst - 1])
                        g.dma_start(out=wsl[1][:, :],
                                    in_=prefetch_ap[1]).then_inc(sems["w"], 16)
                        sems.c["w"] += 32
                if wmode == "cast":
                    b.gpsimd(_load_body)
                elif wmode == "bf16":
                    b.sync(_load_body)
                else:
                    @b.sync
                    def _(e):
                        qi = 0
                        for s2 in range(nst):
                            if s2 >= 2:
                                e.wait_ge(sems["mm"], mm_after[s2 - 2])
                                if wstore is not None:
                                    e.wait_ge(sems["st"], wb0 + (s2 - 1) * 16)
                            for q in range(nq):
                                if qi >= 2:
                                    e.wait_ge(sems["wc"], wcq0 + qi - 1)
                                e.dma_start(out=wstg[qi % 2],
                                            in_=wT_ap[s2][:, q * 1024:(q + 1) * 1024]
                                            ).then_inc(sems["w"], 16)
                                qi += 1

                    @b.vector
                    def _(v):
                        qi = 0
                        for s2 in range(nst):
                            slot = wsl[s2 % 2]
                            for q in range(nq):
                                v.wait_ge(sems["w"], w0 + (qi + 1) * 16)
                                v.tensor_copy(slot[:, q * 1024:(q + 1) * 1024],
                                              wstg[qi % 2]).then_inc(sems["wc"], 1)
                                qi += 1

                if wstore is not None:
                    def _store_body(e):
                        for s2 in range(nst):
                            e.wait_ge(sems["mm"], mm_after[s2])
                            e.dma_start(out=wstore[s2],
                                        in_=wsl[s2 % 2][:, :]).then_inc(sems["st"], 16)
                    if wmode == "castv":
                        b.scalar(_store_body)
                    else:
                        b.sync(_store_body)

                @b.tensor
                def _(t):
                    for s2 in range(nst):
                        if wmode == "castv":
                            t.wait_ge(sems["wc"], wcq0 + (s2 + 1) * nq)
                        else:
                            t.wait_ge(sems["w"], w_after[s2])
                        if s2 >= 2:
                            t.wait_ge(sems["ev"], ev_after[s2 - 2])
                        if rhs_evs is not None and s2 == 1:
                            t.wait_ge(sems["ev"], rhs_evs[-1])
                        if s2 == 0 and entry_ve is not None:
                            t.wait_ge(sems["ve"], entry_ve)
                        if s2 == 0 and rhs_evs is None and entry_ve is None and blk is not None:
                            t.wait_ge(sems["ev"], ev0)
                        slot = wsl[s2 % 2]
                        bi = bank_base + (0 if s2 % 2 == 0 else 2)
                        for m in range(mg):
                            mi = s2 * mg + m
                            for ci in range(nr):
                                bank = PS[bi + m * nr + ci]
                                for k in range(ktiles):
                                    if rhs_evs is not None and s2 == 0 and m == 0 and ci == 0:
                                        t.wait_ge(sems["ev"], rhs_evs[min(k, len(rhs_evs) - 1)])
                                    ins = t.matmul(bank[:, 0:ncols],
                                                   slot[:, k * sc + m * 128:
                                                        k * sc + (m + 1) * 128],
                                                   rhs_ap[:, k * rstride_ + roffs[ci]:
                                                          k * rstride_ + roffs[ci] + ncols],
                                                   start=(k == 0),
                                                   stop=(k == ktiles - 1 and bias_row is None))
                                if bias_row is not None:
                                    ins = t.matmul(bank[:, 0:ncols],
                                                   bias_row[:, mi * 128:(mi + 1) * 128],
                                                   onesrow[:, 0:ncols],
                                                   start=False, stop=True)
                        ins.then_inc(sems["mm"], 1)

                def _evac_body(e):
                    for s2 in range(nst):
                        if s2 == 0 and evac_guard_mm is not None:
                            e.wait_ge(sems["mm"], evac_guard_mm)
                        e.wait_ge(sems["mm"], mm_after[s2])
                        bi = bank_base + (0 if s2 % 2 == 0 else 2)
                        for m in range(mg):
                            for ci in range(nr):
                                bank = PS[bi + m * nr + ci]
                                ins = evac_fn(e, s2 * mg + m, bank[:, 0:ncols], ci)
                        ins.then_inc(sems["ev"], 1)
                if evac_engine == "scalar":
                    b.scalar(_evac_body)
                else:
                    b.vector(_evac_body)
            return ret_ev

        def emit_ffn_sublayer(first):
            xn = A1(0, NF * TC)
            hbuf = A1(8192, NFF * TC)
            wsl = (A1(24576, 4096), A1(28672, 4096))
            wstg[0] = A1(32768, 2048, F32)
            wstg[1] = A1(34816, 2048, F32)
            out0 = A2(0, NF * TC)
            out1 = A2(8192, NF * TC)
            msku = A2(16384, 256, U8)
            b0br_s = arena2[0:1, 16896:16896 + HID]
            b1br_s = arena2[0:1, 19072:19072 + HID]
            with nc.Block(no_gpsimd_drain=True) as b:
                @b.gpsimd
                def _(g):
                    g.dma_start(out=b0br_s, in_=b0b_r[:, :]).then_inc(*sems.inc("w", 16))
                    g.dma_start(out=b1br_s, in_=b1b_r[:, :]).then_inc(*sems.inc("w", 16))
                    g.wait_ge(sems["w"], sems.val("w"))
            for tci, tc0 in enumerate(range(0, T, TC)):
                cast_pass = first and tci == 0
                wm = "cast" if cast_pass else "bf16"
                W0aS = W0aT if cast_pass else W0ab.ap()
                W0bS = W0bT if cast_pass else W0bb.ap()
                W1aS = W1aT if cast_pass else W1ab.ap()
                W1bS = W1bT if cast_pass else W1bb.ap()
                nxtW0a = W0aT if (first and tci == 0) else W0ab.ap()
                emit_ln_chunk(tc0, xn)
                with nc.Block(no_gpsimd_drain=True) as blkc:
                    @blkc.tensor
                    def _(t):
                        for k in range(NF):
                            ins = t.matmul(PS[4][0:1, 0:TC], wgdiff[:, k:k + 1],
                                           xn[:, k * TC:(k + 1) * TC],
                                           start=(k == 0), stop=(k == NF - 1))
                        ins.then_inc(*sems.inc("mm"))

                    @blkc.vector
                    def _(v):
                        v.wait_ge(sems["mm"], sems.val("mm"))
                        v.tensor_copy(rowC[:, :], PS[4][0:1, 0:TC]).then_inc(*sems.inc("ve"))

                    @blkc.tensor
                    def _(t):
                        t.wait_ge(sems["ve"], sems.val("ve"))
                        t.matmul(PS[5][:, 0:TC], ones1b[:, :], rowC[:, :],
                                 start=True, stop=True).then_inc(*sems.inc("mm"))

                    @blkc.vector
                    def _(v):
                        v.wait_ge(sems["mm"], sems.val("mm"))
                        v.tensor_scalar(msku[:, 0:TC], PS[5][:, 0:TC], 0.0, None,
                                        ALU.is_ge).then_inc(*sems.inc("ve"))

                    ve_gate = sems.val("ve")

                    def evac_h0(e, mi, ps, ci):
                        return e.activation(hbuf[:, mi * TC:(mi + 1) * TC], ps, AF.Gelu,
                                            bias=b0a_s[:, mi:mi + 1])

                    ev_h = stream_matmul(W0aS, HID, NFF, xn, TC, wsl, 2, 2, evac_h0,
                                         wmode=wm, carry_in=(tci > 0),
                                         prefetch_ap=W0bS, blk=blkc, bank_base=0,
                                         wstore=W0ab.ap() if cast_pass else None)
                    ev_hk = [ev_h[k // 2] for k in range(NFF)]
                    mm_a = sems.val("mm")

                    def evac_o0(e, mi, ps, ci):
                        return e.activation(out0[:, mi * TC:(mi + 1) * TC], ps, AF.Copy,
                                            scale=0.5)

                    stream_matmul(W0bS, FFD, NF, hbuf, TC, wsl, 1, 4, evac_o0,
                                  bias_row=b0br_s, wmode=wm, carry_in=True,
                                  prefetch_ap=W1aS, rhs_evs=ev_hk, blk=blkc,
                                  bank_base=4, entry_ve=ve_gate,
                                  wstore=W0bb.ap() if cast_pass else None)
                    mm_b0 = sems.val("mm")

                    def evac_h1(e, mi, ps, ci):
                        return e.activation(hbuf[:, mi * TC:(mi + 1) * TC], ps, AF.Relu,
                                            bias=b1a_s[:, mi:mi + 1])

                    ev_h = stream_matmul(W1aS, HID, NFF, xn, TC, wsl, 2, 2, evac_h1,
                                         wmode=wm, carry_in=True, prefetch_ap=W1bS,
                                         blk=blkc, bank_base=0,
                                         evac_guard_mm=mm_b0,
                                         wstore=W1ab.ap() if cast_pass else None)
                    ev_hk = [ev_h[k // 2] for k in range(NFF)]

                    def evac_o1(e, mi, ps, ci):
                        return e.activation(out1[:, mi * TC:(mi + 1) * TC], ps, AF.Copy,
                                            scale=0.5)

                    stream_matmul(W1bS, FFD, NF, hbuf, TC, wsl, 1, 4, evac_o1,
                                  bias_row=b1br_s, wmode=wm, carry_in=True,
                                  rhs_evs=ev_hk, blk=blkc, bank_base=4,
                                  prefetch_ap=W0ab.ap() if tci == 0 else None,
                                  wstore=W1bb.ap() if cast_pass else None)

                    ev_fin = sems.val("ev")

                    @blkc.vector
                    def _(v):
                        v.wait_ge(sems["ev"], ev_fin)
                        for f in range(NF):
                            sl = hbuf[:, f * TC:(f + 1) * TC]
                            v.select(sl, msku[:, 0:TC], out0[:, f * TC:(f + 1) * TC],
                                     out1[:, f * TC:(f + 1) * TC])
                            ins = v.tensor_tensor(resid[:, f * T + tc0:f * T + tc0 + TC],
                                                  resid[:, f * T + tc0:f * T + tc0 + TC],
                                                  sl, ALU.add)
                        ins.then_inc(*sems.inc("ve"))

        def a1ap(off, dims):
            return bass.AP(arena1[:, :].tensor, off, [[A1C, 128]] + dims)

        def a2ap(off, dims):
            return bass.AP(arena2[:, :].tensor, off, [[A2C, 128]] + dims)

        KOFF = 16 * T
        VOFF = 20 * T

        def qslice(h, j):
            return arena2[:, h * T + j * C: h * T + j * C + C]

        def kslice(kv, j):
            return arena2[:, KOFF + kv * T + j * C: KOFF + kv * T + j * C + C]

        def vslice(kv, j):
            return arena2[:, VOFF + kv * T + j * C: VOFF + kv * T + j * C + C]

        RAWOFF = 8192

        def rawt(q):
            return A1(RAWOFF + q * 1027, 1027)

        def emit_p2():
            xn = A1(0, NF * TC)
            wslq = (A1(32840, 2048), A1(34888, 2048))
            for tci, tc0 in enumerate(range(0, T, TC)):
                emit_ln_chunk(tc0, xn)

                def evac_qkv(e, mi, ps, ci, tc0=tc0):
                    return e.activation(rawt(mi)[:, 3 + tc0:3 + tc0 + TC], ps, AF.Copy)

                stream_matmul(qkvT if tci == 0 else qkvb.ap(), HID, NQT, xn, TC,
                              wslq, 1, 2, evac_qkv,
                              wmode="cast" if tci == 0 else "bf16",
                              wstore=qkvb.ap() if tci == 0 else None)

                with nc.Block(no_gpsimd_drain=True) as b:
                    @b.tensor
                    def _(t):
                        for k in range(NF):
                            ins = t.matmul(PS[4][0:16, 0:TC], bT_s[:, k * H:(k + 1) * H],
                                           xn[:, k * TC:(k + 1) * TC],
                                           start=(k == 0), stop=(k == NF - 1))
                        for k in range(NF):
                            ins = t.matmul(PS[5][0:16, 0:TC], gkT_s[:, k * H:(k + 1) * H],
                                           xn[:, k * TC:(k + 1) * TC],
                                           start=(k == 0), stop=(k == NF - 1))
                        ins.then_inc(*sems.inc("mm"))

                    @b.scalar
                    def _(e, tc0=tc0):
                        e.wait_ge(sems["mm"], sems.val("mm"))
                        e.activation(befm[:, tc0:tc0 + TC], PS[4][0:16, 0:TC], AF.Sigmoid)
                        e.activation(scr16[:, 0:TC], PS[5][0:16, 0:TC], AF.Exp, scale=-1.0)
                        e.activation(spfm[:, tc0:tc0 + TC], scr16[:, 0:TC], AF.Ln,
                                     bias=1.0).then_inc(*sems.inc("ev"))

            with nc.Block(no_gpsimd_drain=True) as b:
                @b.gpsimd
                def _(g):
                    g.dma_start(out=cc1_in.ap().rearrange("p (q c) -> p q c", c=3),
                                in_=a1ap(RAWOFF + 1024, [[1027, 24], [1, 3]])).then_inc(*sems.inc("xd", 16))
                    g.wait_ge(sems["xd"], sems.val("xd"))
                    g.collective_compute("AllGather", ALU.bypass, replica_groups=RG,
                                         ins=[cc1_in.ap().opt()],
                                         outs=[cc1_out.ap().opt()]).then_inc(*sems.inc("cc"))

            with nc.Block(no_gpsimd_drain=True) as b:
                cscr = [A1(32840, 2048, F32), A1(34888, 2048, F32)]
                ve0 = sems.val("ve")
                ev0 = sems.val("ev")
                gc0 = sems.val("gc")

                @b.vector
                def _(v):
                    for q in range(NQT):
                        if q >= 2:
                            v.wait_ge(sems["ev"], ev0 + q - 1)
                        cs = cscr[q % 2][:, 0:1021]
                        rt = rawt(q)
                        v.tensor_scalar(cs, rt[:, 6:1027], convw_s[:, q * 4 + 3:q * 4 + 4],
                                        None, ALU.mult)
                        for i in range(3):
                            sh = 3 - i
                            ins = v.scalar_tensor_tensor(cs, rt[:, 6 - sh:1027 - sh],
                                                         convw_s[:, q * 4 + i:q * 4 + i + 1],
                                                         cs, ALU.mult, ALU.add)
                        ins.then_inc(sems["ve"], 1)
                sems.c["ve"] = ve0 + NQT

                @b.scalar
                def _(e):
                    for q in range(NQT):
                        e.wait_ge(sems["ve"], ve0 + q + 1)
                        e.activation(arena2[:, q * T + 3:(q + 1) * T],
                                     cscr[q % 2][:, 0:1021],
                                     AF.Silu).then_inc(sems["ev"], 1)
                sems.c["ev"] = ev0 + NQT

            with nc.Block(no_gpsimd_drain=True) as b:
                @b.gpsimd
                def _(g):
                    g.wait_ge(sems["cc"], sems.val("cc"))
                    g.dma_start(out=a1ap(RAWOFF, [[1027, 24], [1, 3]]),
                                in_=cc1_out.ap()[0:128, :].rearrange("p (q c) -> p q c", c=3)).then_inc(*sems.inc("xd", 16))
                    g.wait_ge(sems["xd"], sems.val("xd"))

                @b.vector
                def _(v):
                    v.wait_ge(sems["xd"], sems.val("xd"))
                    v.tensor_scalar(a1ap(RAWOFF, [[1027, 24], [1, 3]]),
                                    a1ap(RAWOFF, [[1027, 24], [1, 3]]),
                                    halfm_s[:, 0:1], None, ALU.mult)
                    cs = cscr[0]
                    for q in range(NQT):
                        rt = rawt(q)
                        v.tensor_scalar(cs[:, q * 4:q * 4 + 3], rt[:, 3:6],
                                        convw_s[:, q * 4 + 3:q * 4 + 4], None, ALU.mult)
                        for i in range(3):
                            sh = 3 - i
                            ins = v.scalar_tensor_tensor(cs[:, q * 4:q * 4 + 3],
                                                         rt[:, 3 - sh:6 - sh],
                                                         convw_s[:, q * 4 + i:q * 4 + i + 1],
                                                         cs[:, q * 4:q * 4 + 3],
                                                         ALU.mult, ALU.add)
                    ins.then_inc(*sems.inc("ve"))

                @b.scalar
                def _(e):
                    e.wait_ge(sems["ve"], sems.val("ve"))
                    for q in range(NQT):
                        ins = e.activation(arena2[:, q * T:q * T + 3],
                                           cscr[0][:, q * 4:q * 4 + 3], AF.Silu)
                    ins.then_inc(*sems.inc("ev"))

            with nc.Block(no_gpsimd_drain=True) as b:
                qsq = [A1(32840, 512), A1(33352, 512)]
                rowBs = [rowB, A1(33864, 1024, F32)]
                rowDs = [rowD, A1(34888, 512)]
                LS = [("mm", "ev", "ve"), ("mmB", "evB", "veB")]
                iters = [(tl, tc) for tl in range(20) for tc in range(NTC)]

                def l2_steps(it, tl, tc):
                    L = it % 2
                    smm, sev, sve = LS[L]
                    sl = arena2[:, tl * T + tc * TC: tl * T + tc * TC + TC]
                    bank = PS[4 + L]
                    bank2 = PS[6 + L]
                    qb = qsq[L]
                    rB = rowBs[L]
                    rD = rowDs[L]

                    def sq(v):
                        v.tensor_tensor(qb[:, 0:TC], sl, sl, ALU.mult).then_inc(sems[sve], 1)
                        sems.c[sve] += 1

                    def summ(t):
                        t.wait_ge(sems[sve], sems.val(sve))
                        t.matmul(bank[0:1, 0:TC], onescolb[:, :], qb[:, 0:TC],
                                 start=True, stop=True).then_inc(sems[smm], 1)
                        sems.c[smm] += 1

                    def rowsa(v):
                        v.wait_ge(sems[smm], sems.val(smm))
                        v.tensor_scalar(rB[0:1, 0:TC], bank[0:1, 0:TC], EPS_L2, None, ALU.add)
                        v.reciprocal(rB[0:1, 0:TC], rB[0:1, 0:TC]).then_inc(sems[sve], 1)
                        sems.c[sve] += 1

                    def rows(e):
                        e.wait_ge(sems[sve], sems.val(sve))
                        if tl < 16:
                            e.activation(rD[0:1, 0:TC], rB[0:1, 0:TC], AF.Sqrt, scale=1.0 / 128.0)
                        else:
                            e.activation(rD[0:1, 0:TC], rB[0:1, 0:TC], AF.Sqrt)
                        e.drain().then_inc(sems[sev], 1)
                        sems.c[sev] += 1

                    def bcast(t):
                        t.wait_ge(sems[sev], sems.val(sev))
                        t.matmul(bank2[:, 0:TC], ones1b[:, :], rD[0:1, 0:TC],
                                 start=True, stop=True).then_inc(sems[smm], 1)
                        sems.c[smm] += 1

                    def apply(v):
                        v.wait_ge(sems[smm], sems.val(smm))
                        v.tensor_tensor(sl, sl, bank2[:, 0:TC], ALU.mult).then_inc(sems[sve], 1)
                        sems.c[sve] += 1

                    return [("vector", sq), ("tensor", summ), ("vector", rowsa),
                            ("scalar", rows), ("tensor", bcast), ("vector", apply)]

                for pi in range(0, len(iters), 2):
                    st0 = l2_steps(pi, *iters[pi])
                    st1 = l2_steps(pi + 1, *iters[pi + 1]) if pi + 1 < len(iters) else None
                    for si in range(6):
                        eng, fn = st0[si]
                        getattr(b, eng)(fn)
                        if st1 is not None:
                            eng, fn = st1[si]
                            getattr(b, eng)(fn)
        A4o, Vt4o, Kt4o, Ngo = 0, 512, 1024, 1536
        Mpo = 2048
        Nto = 5632
        Xo = 6144
        Gto = 10240
        Qeo = 12288
        Mto = 14336
        Tso = 16384
        Pho = 18432
        Info = 20480

        def Xh(h):
            return A1(Xo + h * 256, 256)

        def emit_p3_chunk(j):
            LS = [("mm", "ev", "ve"), ("mmB", "evB", "veB")]

            def lane_sched(g):
                L = g % 2
                smm, sev, sve = LS[L]
                bksq = PS[3 + L]
                bkap = PS[5 + L]
                base = 1536 + L * 2304
                Ngo_ = base
                Mpo_ = base + 256
                Nto_ = base + 2048
                h0 = 2 * g
                steps = []

                def nmask(v):
                    v.wait_ge(sems["ev"], kv_ev)
                    for q in range(2):
                        h = h0 + q
                        ins = v.scalar_tensor_tensor(
                            A1(Ngo_ + q * 128, 128), A1(A4o + (h // 4) * 128, 128),
                            be_col[:, j * 16 + h:j * 16 + h + 1], trilS[:, :],
                            ALU.mult, ALU.mult)
                    ins.then_inc(sems[sve], 1)
                    sems.c[sve] += 1
                steps.append(("vector", nmask))

                def m1(t):
                    t.wait_ge(sems[sve], sems.val(sve))
                    for q in range(2):
                        ins = t.matmul(bksq[:, q * 128:(q + 1) * 128],
                                       A1(Ngo_ + q * 128, 128), identb[:, :],
                                       start=True, stop=True)
                    ins.then_inc(sems[smm], 1)
                    sems.c[smm] += 1
                steps.append(("tensor", m1))

                def m1e(v):
                    v.wait_ge(sems[smm], sems.val(smm))
                    v.tensor_copy(A1(Mpo_, 256),
                                  bksq[:, 0:256]).then_inc(sems[sve], 1)
                    sems.c[sve] += 1
                steps.append(("vector", m1e))

                for lv in range(1, 7):
                    def sq(t, lv=lv):
                        t.wait_ge(sems[sve], sems.val(sve))
                        for q in range(2):
                            srcn = A1(Ngo_ + q * 128, 128) if lv == 1 else A1(Nto_ + q * 128, 128)
                            ins = t.matmul(bksq[:, q * 128:(q + 1) * 128], srcn,
                                           A1(Mpo_ + (lv - 1) * 256 + q * 128, 128),
                                           start=True, stop=True)
                        ins.then_inc(sems[smm], 1)
                        sems.c[smm] += 1
                    steps.append(("tensor", sq))

                    def sqe(v, lv=lv):
                        v.wait_ge(sems[smm], sems.val(smm))
                        v.tensor_copy(A1(Mpo_ + lv * 256, 256),
                                      bksq[:, 0:256]).then_inc(sems[sve], 1)
                        sems.c[sve] += 1
                    steps.append(("vector", sqe))

                    if lv < 6:
                        def tr(t, lv=lv):
                            t.wait_ge(sems[sve], sems.val(sve))
                            for q in range(2):
                                ins = t.matmul(bksq[:, 256 + q * 128:256 + (q + 1) * 128],
                                               A1(Mpo_ + lv * 256 + q * 128, 128),
                                               identb[:, :], start=True, stop=True)
                            ins.then_inc(sems[smm], 1)
                            sems.c[smm] += 1
                        steps.append(("tensor", tr))

                        def tre(v):
                            v.wait_ge(sems[smm], sems.val(smm))
                            v.tensor_copy(A1(Nto_, 256),
                                          bksq[:, 256:512]).then_inc(sems[sve], 1)
                            sems.c[sve] += 1
                        steps.append(("vector", tre))

                def xinit(v):
                    v.wait_ge(sems[sve], sems.val(sve))
                    for q in range(2):
                        h = h0 + q
                        kv = h // 4
                        v.tensor_scalar(Xh(h)[:, 0:128], A1(Vt4o + kv * 128, 128),
                                        r_col[:, j * 16 + h:j * 16 + h + 1], None, ALU.mult)
                        ins = v.tensor_scalar(Xh(h)[:, 128:256], A1(Kt4o + kv * 128, 128),
                                              be_col[:, j * 16 + h:j * 16 + h + 1],
                                              None, ALU.mult)
                    ins.then_inc(sems[sve], 1)
                    sems.c[sve] += 1
                steps.append(("vector", xinit))

                for lv in range(6, 0, -1):
                    def ap(t, lv=lv):
                        t.wait_ge(sems[sve], sems.val(sve))
                        for q in range(2):
                            h = h0 + q
                            co = q * 256
                            ins = t.matmul(bkap[:, co:co + 256],
                                           A1(Mpo_ + lv * 256 + q * 128, 128), Xh(h),
                                           start=True, stop=True)
                        ins.then_inc(sems[smm], 1)
                        sems.c[smm] += 1
                    steps.append(("tensor", ap))

                    def ape(v):
                        v.wait_ge(sems[smm], sems.val(smm))
                        v.scalar_tensor_tensor(A1(Xo + h0 * 256, 512), bkap[:, :],
                                               1.0, A1(Xo + h0 * 256, 512),
                                               ALU.mult, ALU.add).then_inc(sems[sve], 1)
                        sems.c[sve] += 1
                    steps.append(("vector", ape))

                def fin(t):
                    t.wait_ge(sems[sve], sems.val(sve))
                    for q in range(2):
                        h = h0 + q
                        ins = t.matmul(bkap[:, q * 256:(q + 1) * 256],
                                       A1(Mpo_ + q * 128, 128),
                                       Xh(h), start=True, stop=True)
                    ins.then_inc(sems[smm], 1)
                    sems.c[smm] += 1
                steps.append(("tensor", fin))

                def fine(v):
                    v.wait_ge(sems[smm], sems.val(smm))
                    v.tensor_tensor(A1(Xo + h0 * 256, 512), A1(Xo + h0 * 256, 512),
                                    bkap[:, :], ALU.subtract).then_inc(sems[sve], 1)
                    sems.c[sve] += 1
                steps.append(("vector", fine))

                def gt(t):
                    t.wait_ge(sems[sve], sems.val(sve))
                    for q in range(2):
                        h = h0 + q
                        ins = t.matmul(bkap[:, q * 128:(q + 1) * 128],
                                       kslice(h // 4, j), qslice(h, j),
                                       start=True, stop=True)
                    ins.then_inc(sems[smm], 1)
                    sems.c[smm] += 1
                steps.append(("tensor", gt))

                def gte(v, g=g):
                    v.wait_ge(sems[smm], sems.val(smm))
                    v.tensor_tensor(A1(Gto + h0 * 128, 256), bkap[:, 0:256],
                                    triuI4[:, 0:256], ALU.mult).then_inc(sems[sve], 1)
                    sems.c[sve] += 1
                steps.append(("vector", gte))

                def qe(t):
                    t.wait_ge(sems[sve], sems.val(sve))
                    for q in range(2):
                        h = h0 + q
                        ins = t.matmul(bkap[:, q * 128:(q + 1) * 128],
                                       Xh(h)[:, 128:256], A1(Gto + h * 128, 128),
                                       start=True, stop=True)
                    ins.then_inc(sems[smm], 1)
                    sems.c[smm] += 1
                steps.append(("tensor", qe))

                def qee(v):
                    v.wait_ge(sems[smm], sems.val(smm))
                    qv = a2ap(h0 * T + j * C, [[T, 2], [1, 128]])
                    v.tensor_tensor(A1(Qeo + h0 * 128, 256).rearrange(
                        "p (h c) -> p h c", c=128), qv,
                        bkap[:, 0:256].rearrange("p (h c) -> p h c", c=128),
                        ALU.subtract).then_inc(sems[sve], 1)
                    sems.c[sve] += 1
                steps.append(("vector", qee))

                def mt(t):
                    t.wait_ge(sems[sve], sems.val(sve))
                    for q in range(2):
                        h = h0 + q
                        kv = h // 4
                        ins = t.matmul(bkap[:, q * 128:(q + 1) * 128],
                                       Xh(h)[:, 128:256], A1(Kt4o + kv * 128, 128),
                                       start=True, stop=True)
                    ins.then_inc(sems[smm], 1)
                    sems.c[smm] += 1
                steps.append(("tensor", mt))

                def mte(v):
                    v.wait_ge(sems[smm], sems.val(smm))
                    idr = identf[:, :].unsqueeze(1).broadcast_to([128, 2, 128])
                    v.tensor_tensor(A1(Mto + h0 * 128, 256).rearrange(
                        "p (h c) -> p h c", c=128), idr,
                        bkap[:, 0:256].rearrange("p (h c) -> p h c", c=128),
                        ALU.subtract).then_inc(sems[sve], 1)
                    sems.c[sve] += 1
                steps.append(("vector", mte))

                return steps

            with nc.Block(no_gpsimd_drain=True) as b:
                @b.tensor
                def _(t):
                    for kv in range(4):
                        t.matmul(PS[0][:, kv * 128:(kv + 1) * 128], kslice(kv, j),
                                 kslice(kv, j), start=True, stop=True)
                        t.matmul(PS[1][:, kv * 128:(kv + 1) * 128], vslice(kv, j),
                                 identb[:, :], start=True, stop=True)
                        ins = t.matmul(PS[2][:, kv * 128:(kv + 1) * 128], kslice(kv, j),
                                       identb[:, :], start=True, stop=True)
                    ins.then_inc(*sems.inc("mm"))

                @b.scalar
                def _(e):
                    e.wait_ge(sems["mm"], sems.val("mm"))
                    e.activation(A1(A4o, 512), PS[0][:, :], AF.Copy)
                    e.activation(A1(Vt4o, 512), PS[1][:, :], AF.Copy)
                    ins = e.activation(A1(Kt4o, 512), PS[2][:, :], AF.Copy)
                    ins.then_inc(*sems.inc("ev"))

                kv_ev = sems.val("ev")

                sched0 = []
                sched1 = []
                for g in range(8):
                    (sched0 if g % 2 == 0 else sched1).append(g)
                q0 = [s for g in sched0 for s in lane_sched(g)] if False else None
                its0 = iter(sched0)
                its1 = iter(sched1)
                cur0 = lane_sched(next(its0))
                cur1 = lane_sched(next(its1))
                i0_, i1_ = 0, 0
                while cur0 is not None or cur1 is not None:
                    for lane in (0, 1):
                        if lane == 0 and cur0 is not None:
                            eng, fn = cur0[i0_]
                            getattr(b, eng)(fn)
                            i0_ += 1
                            if i0_ == len(cur0):
                                g_ = next(its0, None)
                                cur0 = lane_sched(g_) if g_ is not None else None
                                i0_ = 0
                        elif lane == 1 and cur1 is not None:
                            eng, fn = cur1[i1_]
                            getattr(b, eng)(fn)
                            i1_ += 1
                            if i1_ == len(cur1):
                                g_ = next(its1, None)
                                cur1 = lane_sched(g_) if g_ is not None else None
                                i1_ = 0

                for g in range(4):
                    h0 = 4 * g

                    def _ch(t, h0=h0):
                        t.wait_ge(sems["ve"], sems.val("ve"))
                        t.wait_ge(sems["veB"], sems.val("veB"))
                        t.wait_ge(sems["ev"], sems.val("ev"))
                        for q in range(4):
                            h = h0 + q
                            kv = h // 4
                            c0 = q * 128
                            t.matmul(PS[7][:, c0:c0 + 128], A1(Qeo + h * 128, 128),
                                     A1(Tso + h * 128, 128), start=True, stop=False)
                            t.matmul(PS[7][:, c0:c0 + 128], A1(Gto + h * 128, 128),
                                     Xh(h)[:, 0:128], start=False, stop=True)
                            t.matmul(PS[3][:, c0:c0 + 128], A1(Pho + h * 128, 128),
                                     A1(Qeo + h * 128, 128), start=True, stop=True)
                            t.matmul(PS[5][:, c0:c0 + 128], A1(Mto + h * 128, 128),
                                     A1(Tso + h * 128, 128), start=True, stop=False)
                            t.matmul(PS[5][:, c0:c0 + 128], A1(Kt4o + kv * 128, 128),
                                     Xh(h)[:, 0:128], start=False, stop=True)
                            ins = t.matmul(PS[6][:, c0:c0 + 128], A1(Mto + h * 128, 128),
                                           A1(Pho + h * 128, 128), start=True, stop=True)
                        ins.then_inc(sems["mm"], 1)
                    b.tensor(_ch)
                    sems.c["mm"] += 1

                    def _che(v, h0=h0, g=g):
                        v.wait_ge(sems["mm"], sems.val("mm"))
                        for q in range(4):
                            h = h0 + q
                            c0 = q * 128
                            v.tensor_scalar(qslice(h, j), PS[7][:, c0:c0 + 128],
                                            a_col[:, j * 16 + h:j * 16 + h + 1],
                                            None, ALU.mult)
                            v.tensor_scalar(A1(Tso + h * 128, 128), PS[5][:, c0:c0 + 128],
                                            aC_col[:, j * 16 + h:j * 16 + h + 1],
                                            None, ALU.mult)
                            ins = v.tensor_scalar(A1(Pho + h * 128, 128),
                                                  PS[6][:, c0:c0 + 128],
                                                  aC_col[:, j * 16 + h:j * 16 + h + 1],
                                                  None, ALU.mult)
                        ins.then_inc(sems["ve"], 1)
                    b.vector(_che)
                    sems.c["ve"] += 1

                    def _chei(e, g=g):
                        e.wait_ge(sems["mm"], sems.val("mm"))
                        ins = e.activation(A1(Info + j * 2048 + g * 512, 512), PS[3][:, :],
                                           AF.Copy)
                        ins.then_inc(sems["ev"], 1)
                    b.scalar(_chei)
                    sems.c["ev"] += 1

        def emit_p3():
            sptmb = A1(36864, 16)
            acolb = A1(36880, 16)
            rowacb = A1(36896, 16)
            rinvs = A1(36912, 32, F32)
            with nc.Block(no_gpsimd_drain=True) as b:
                @b.vector
                def _(v):
                    v.memset(A1(Tso, 2048), 0.0)
                    v.tensor_copy(A1(Pho, 2048).rearrange("p (h c) -> p h c", c=128),
                                  identb[:, :].unsqueeze(1).broadcast_to([128, 16, 128]))
                    v.nop().then_inc(*sems.inc("ve"))

                for j in range(NCH):
                    bk = PS[4 + (j % 2)]
                    bk2 = PS[6 + (j % 2)]

                    def _v1(t, j=j, bk=bk):
                        t.wait_ge(sems["ve"], sems.val("ve"))
                        t.matmul(bk[0:128, 0:16], spfm[:, j * C:(j + 1) * C],
                                 identb[0:16, 0:16], start=True, stop=True)
                        t.matmul(bk[0:128, 16:32], befm[:, j * C:(j + 1) * C],
                                 identb[0:16, 0:16], start=True, stop=True).then_inc(sems["mm"], 1)
                    b.tensor(_v1)
                    sems.c["mm"] += 1

                    def _e1(v, j=j, bk=bk):
                        v.wait_ge(sems["mm"], sems.val("mm"))
                        v.tensor_copy(sptmb, bk[0:128, 0:16])
                        v.tensor_copy(be_col[:, j * 16:(j + 1) * 16],
                                      bk[0:128, 16:32]).then_inc(sems["ve"], 1)
                    b.vector(_e1)
                    sems.c["ve"] += 1

                    def _t2(t, bk2=bk2):
                        t.wait_ge(sems["ve"], sems.val("ve"))
                        t.matmul(bk2[0:128, 0:16], triuN16[:, :], sptmb,
                                 start=True, stop=True).then_inc(sems["mm"], 1)
                    b.tensor(_t2)
                    sems.c["mm"] += 1

                    def _e2(e, j=j, bk2=bk2):
                        e.wait_ge(sems["mm"], sems.val("mm"))
                        e.activation(a_col[:, j * 16:(j + 1) * 16], bk2[0:128, 0:16], AF.Exp)
                        e.activation(acolb, bk2[0:128, 0:16], AF.Exp)
                        e.activation(rinvs[:, 0:16], bk2[0:128, 0:16], AF.Exp,
                                     scale=-1.0).then_inc(sems["ev"], 1)
                    b.scalar(_e2)
                    sems.c["ev"] += 1

                    def _v2(v, j=j):
                        v.wait_ge(sems["ev"], sems.val("ev"))
                        v.tensor_tensor(r_col[:, j * 16:(j + 1) * 16],
                                        be_col[:, j * 16:(j + 1) * 16],
                                        rinvs[:, 0:16], ALU.mult).then_inc(sems["ve"], 1)
                    b.vector(_v2)
                    sems.c["ve"] += 1

                    def _t3(t, bk=bk):
                        t.wait_ge(sems["ve"], sems.val("ve"))
                        t.matmul(bk[0:1, 32:48], identb[:, 127:128], acolb,
                                 start=True, stop=True).then_inc(sems["mm"], 1)
                    b.tensor(_t3)
                    sems.c["mm"] += 1

                    def _e3(v, bk=bk):
                        v.wait_ge(sems["mm"], sems.val("mm"))
                        v.tensor_copy(rowacb[0:1, :], bk[0:1, 32:48]).then_inc(sems["ve"], 1)
                    b.vector(_e3)
                    sems.c["ve"] += 1

                    def _t4(t, bk=bk):
                        t.wait_ge(sems["ve"], sems.val("ve"))
                        t.matmul(bk[0:128, 48:64], ones1b[:, :], rowacb[0:1, :],
                                 start=True, stop=True).then_inc(sems["mm"], 1)
                    b.tensor(_t4)
                    sems.c["mm"] += 1

                    def _e4(v, j=j, bk=bk):
                        v.wait_ge(sems["mm"], sems.val("mm"))
                        v.tensor_copy(aC_col[:, j * 16:(j + 1) * 16],
                                      bk[0:128, 48:64]).then_inc(sems["ve"], 1)
                    b.vector(_e4)
                    sems.c["ve"] += 1

            for j in range(NCH):
                emit_p3_chunk(j)

            with nc.Block(no_gpsimd_drain=True) as b:
                @b.gpsimd
                def _(g):
                    g.dma_start(out=cc2_in.ap(), in_=A1(Tso, 2048)).then_inc(*sems.inc("xd", 16))
                    g.wait_ge(sems["xd"], sems.val("xd"))
                    g.collective_compute("AllGather", ALU.bypass, replica_groups=RG,
                                         ins=[cc2_in.ap().opt()],
                                         outs=[cc2_out.ap().opt()]).then_inc(*sems.inc("cc"))
                    g.wait_ge(sems["cc"], sems.val("cc"))
                    g.dma_start(out=A1(Tso, 2048),
                                in_=cc2_out.ap()[0:128, :]).then_inc(*sems.inc("xd", 16))
                    g.wait_ge(sems["xd"], sems.val("xd"))

                @b.vector
                def _(v):
                    v.wait_ge(sems["xd"], sems.val("xd"))
                    v.tensor_scalar(A1(Tso, 2048), A1(Tso, 2048), halfm_s[:, 0:1],
                                    None, ALU.mult).then_inc(*sems.inc("ve"))

            with nc.Block(no_gpsimd_drain=True) as b:
                for j in range(NCH):
                    for g in range(4):
                        h0 = 4 * g
                        bank = PS[4 + ((j * 4 + g) % 2)]

                        def _pb(t, j=j, h0=h0, bank=bank):
                            if sems.val("mm") > 0:
                                t.wait_ge(sems["ve"], sems.val("ve"))
                            for q in range(4):
                                h = h0 + q
                                ins = t.matmul(bank[:, q * 128:(q + 1) * 128],
                                               A1(Info + j * 2048 + h * 128, 128),
                                               A1(Tso + h * 128, 128),
                                               start=True, stop=True)
                            ins.then_inc(sems["mm"], 1)
                        b.tensor(_pb)
                        sems.c["mm"] += 1

                        def _pbe(v, j=j, h0=h0, bank=bank):
                            v.wait_ge(sems["mm"], sems.val("mm"))
                            for q in range(4):
                                h = h0 + q
                                ins = v.scalar_tensor_tensor(
                                    qslice(h, j), bank[:, q * 128:(q + 1) * 128],
                                    a_col[:, j * 16 + h:j * 16 + h + 1],
                                    qslice(h, j), ALU.mult, ALU.add)
                            ins.then_inc(sems["ve"], 1)
                        b.vector(_pbe)
                        sems.c["ve"] += 1

        def emit_p4():
            xnf = A1(0, NF * T)
            wslg = (A1(16384, 2048), A1(18432, 2048))
            GOFF = 20480

            for tc0 in range(0, T, TC):
                emit_ln_chunk_to(tc0, xnf, T)

            def evac_g(e, mi, ps, ci):
                return e.activation(A1(GOFF + mi * T + ci * TC, TC), ps, AF.Copy)

            stream_matmul(gT, HID, NF, xnf, TC, wslg, 1, 2, evac_g,
                          rstride=T, roffs=(0, TC))

            with nc.Block(no_gpsimd_drain=True) as b:
                @b.scalar
                def _(e):
                    for f in range(NF):
                        ins = e.activation(A1(GOFF + f * T, T), A1(GOFF + f * T, T),
                                           AF.Silu)
                    ins.then_inc(*sems.inc("ev"))

            dump = A1(36864, 128)
            rms2_s = rms_s
            rmsB = A1(36992, 32, F32)
            LSP = [("mm", "ev", "ve"), ("mmB", "evB", "veB")]

            def p4_steps(j):
                L = j % 2
                smm, sev, sve = LSP[L]
                rms = rms_s[:, :] if L == 0 else rmsB
                bk = (PS[4], PS[5]) if L == 0 else (PS[6], PS[7])

                def squares(e):
                    for h in range(H):
                        ins = e.activation(dump, qslice(h, j), AF.Square,
                                           accum_out=rms[:, h:h + 1])
                    ins.then_inc(sems[sev], 1)
                    sems.c[sev] += 1

                def lnexp(e):
                    e.activation(rms, rms, AF.Ln, scale=1.0 / 128.0, bias=EPS_RMS)
                    e.activation(rms, rms, AF.Exp, scale=-0.5).then_inc(sems[sev], 1)
                    sems.c[sev] += 1

                def stt(v):
                    v.wait_ge(sems[sev], sems.val(sev))
                    for h in range(H):
                        ins = v.scalar_tensor_tensor(qslice(h, j), qslice(h, j),
                                                     rms[:, h:h + 1], onw_s[:, :],
                                                     ALU.mult, ALU.mult)
                    ins.then_inc(sems[sve], 1)
                    sems.c[sve] += 1

                steps = [("scalar", squares), ("scalar", lnexp), ("vector", stt)]
                for g in range(4):
                    h0 = g * 4
                    bank = bk[g % 2]

                    def tr(t, h0=h0, bank=bank):
                        t.wait_ge(sems[sve], sems.val(sve))
                        for q in range(4):
                            ins = t.matmul(bank[:, q * 128:(q + 1) * 128],
                                           qslice(h0 + q, j), identb[:, :],
                                           start=True, stop=True)
                        ins.then_inc(sems[smm], 1)
                        sems.c[smm] += 1
                    steps.append(("tensor", tr))

                    def tre(v, h0=h0, bank=bank):
                        v.wait_ge(sems[smm], sems.val(smm))
                        ov = a2ap(h0 * T + j * C, [[T, 4], [1, 128]])
                        gv = a1ap(GOFF + h0 * T + j * C, [[T, 4], [1, 128]])
                        v.tensor_tensor(ov, bank[:, :].rearrange(
                            "p (h c) -> p h c", c=128), gv,
                            ALU.mult).then_inc(sems[sve], 1)
                        sems.c[sve] += 1
                    steps.append(("vector", tre))
                return steps

            with nc.Block(no_gpsimd_drain=True) as b:
                for j0 in range(0, NCH, 2):
                    st0 = p4_steps(j0)
                    st1 = p4_steps(j0 + 1)
                    for si in range(len(st0)):
                        eng, fn = st0[si]
                        getattr(b, eng)(fn)
                        eng, fn = st1[si]
                        getattr(b, eng)(fn)

            wslo = (A1(16384, 2048), A1(18432, 2048))

            def evac_ow(v, mi, ps, ci):
                return v.tensor_tensor(resid[:, mi * T + ci * TC:mi * T + ci * TC + TC],
                                       resid[:, mi * T + ci * TC:mi * T + ci * TC + TC],
                                       ps, ALU.add)

            stream_matmul(oTw, HID, NF, arena2[:, 0:16 * T], TC, wslo, 1, 2,
                          evac_ow, rstride=T, roffs=(0, TC), evac_engine="vector")

        with nc.named_scope("P1ffn"):
            emit_ffn_sublayer(True)

        if _PHASES >= 2:
            with nc.named_scope("P2proj"):
                emit_p2()
        if _PHASES >= 3:
            with nc.named_scope("P3delta"):
                emit_p3()
        if _PHASES >= 4:
            with nc.named_scope("P4out"):
                emit_p4()
        if _PHASES >= 5:
            with nc.named_scope("P5ffn"):
                emit_ffn_sublayer(False)

        if _DBG_QKVC:
            dbg = nc.dram_tensor("dbg", [QKVD, T], F32, kind="ExternalOutput").ap()
            with nc.Block(no_gpsimd_drain=True) as b:
                @b.gpsimd
                def _(g):
                    g.dma_start(out=dbg.rearrange("(q p) t -> p q t", p=128),
                                in_=arena2[:, 0:NQT * T].rearrange("p (q t) -> p q t", q=NQT)).then_inc(*sems.inc("st", 16))
                    g.wait_ge(sems["st"], sems.val("st"))
        if _DBG_STATE:
            dbg2 = nc.dram_tensor("dbg2", [128, H * DK], F32, kind="ExternalOutput").ap()
            dbg3 = nc.dram_tensor("dbg3", [128, H * DK], F32, kind="ExternalOutput").ap()
            with nc.Block(no_gpsimd_drain=True) as b:
                @b.gpsimd
                def _(g):
                    g.dma_start(out=dbg2, in_=A1(Tso, 2048)).then_inc(*sems.inc("st", 16))
                    g.dma_start(out=dbg3, in_=A1(Pho, 2048)).then_inc(*sems.inc("st", 16))
                    g.wait_ge(sems["st"], sems.val("st"))

        with nc.Block(no_gpsimd_drain=True) as b:
            @b.sync
            def _(e):
                e.dma_start(out=out_d.rearrange("(f p) t -> p f t", p=128),
                            in_=resid[:, :].rearrange("p (f t) -> p f t", f=NF)).then_inc(*sems.inc("st", 16))
                e.wait_ge(sems["st"], sems.val("st"))

    return nc


CONSTS = [1.0, EPS_LN, EPS_L2, -0.5, 0.5, float(-0.5 * np.log(128.0)),
          EPS_RMS, 1.0 / 128.0, -1.0, -1.0 / 16.0, 128.0, 0.0]


def _host_inputs(inputs):
    x = np.ascontiguousarray(np.asarray(inputs["L0_attn_hidden_states"]))
    B, Sq, _ = x.shape
    xt = x.reshape(B * Sq, HID)
    def _tile_w(W, mg):
        WT = np.asarray(W).T
        kdim, M = WT.shape
        kt, sc = kdim // 128, mg * 128
        nst = M // sc
        arr = WT.reshape(kt, 128, nst, sc).transpose(2, 1, 0, 3)
        return np.ascontiguousarray(arr).reshape(nst, 128, kt * sc)

    base = {
        "qkvT": _tile_w(inputs["L0_attn_qkv_w"], 1),
        "gT": _tile_w(inputs["L0_attn_g_w"], 1),
        "bT": np.ascontiguousarray(np.asarray(inputs["L0_attn_b_w"]).T),
        "gkT": np.ascontiguousarray(np.asarray(inputs["L0_attn_gk_w"]).T),
        "oT": _tile_w(inputs["L0_attn_o_w"], 1),
        "WgT": np.ascontiguousarray(np.asarray(inputs["L0_ffn_Wg"]).T),
        "W0aT": _tile_w(inputs["L0_ffn_W0a"], 2),
        "W0bT": _tile_w(inputs["L0_ffn_W0b"], 1),
        "W1aT": _tile_w(inputs["L0_ffn_W1a"], 2),
        "W1bT": _tile_w(inputs["L0_ffn_W1b"], 1),
        "convw": np.ascontiguousarray(np.asarray(inputs["L0_attn_conv_w"])),
        "b0a_c": np.ascontiguousarray(np.asarray(inputs["L0_ffn_b0a"]).reshape(NFF, 128).T),
        "b1a_c": np.ascontiguousarray(np.asarray(inputs["L0_ffn_b1a"]).reshape(NFF, 128).T),
        "b0b_r": np.ascontiguousarray(np.asarray(inputs["L0_ffn_b0b"]).reshape(1, HID)),
        "b1b_r": np.ascontiguousarray(np.asarray(inputs["L0_ffn_b1b"]).reshape(1, HID)),
        "onw_b": np.ascontiguousarray(
            np.broadcast_to(np.asarray(inputs["L0_attn_out_norm_w"]), (128, 128))),
    }
    in_maps = []
    for c in range(8):
        bidx, half = c // 2, c % 2
        t0 = bidx * Sq + half * T
        m = dict(base)
        m["x"] = np.ascontiguousarray(xt[t0:t0 + T, :].T)
        m["halfm"] = np.full((128, 1), float(half), np.float32)
        in_maps.append(m)
    return in_maps


def _run(inputs):
    nc = build()
    in_maps = _host_inputs(inputs)
    return run_bass_kernel_spmd(nc, in_maps, core_ids=list(range(8)))


def kernel(**inputs):
    res = _run(inputs)
    B, Sq = 4, 2048
    full = np.empty((B, Sq, HID), np.float32)
    for c in range(8):
        bidx, half = c // 2, c % 2
        full[bidx, half * T:(half + 1) * T, :] = res.results[c]["out"].T
    return full


# version 21
# speedup vs baseline: 1.2259x; 1.0644x over previous
import numpy as np
import concourse.bass as bass
import concourse.mybir as mybir
from concourse.bass_utils import run_bass_kernel_spmd
from contextlib import ExitStack

F32 = mybir.dt.float32
BF16 = mybir.dt.bfloat16
U8 = mybir.dt.uint8
AF = mybir.ActivationFunctionType
ALU = mybir.AluOpType

HID = 2048
T = 1024
NF = HID // 128
TC = 512
NTC = T // TC
H = 16
HKV = 4
DK = 128
QKVD = 3072
NQT = QKVD // 128
FFD = 4096
NFF = FFD // 128
C = 128
NCH = T // C
EPS_LN = 1e-5
EPS_L2 = 1e-6
EPS_RMS = 1e-5

_PHASES = 9
_DBG_QKVC = False
_DBG_STATE = False


class Sems:

    def __init__(self, stack, nc, names):
        self.h = {n: stack.enter_context(nc.semaphore(n)) for n in names}
        self.c = {n: 0 for n in names}

    def inc(self, n, k=1):
        self.c[n] += k
        return self.h[n], k

    def val(self, n):
        return self.c[n]

    def __getitem__(self, n):
        return self.h[n]


def build():
    nc = bass.Bass("TRN2", target_bir_lowering=False)

    def I(name, shape, dtype=F32):
        return nc.dram_tensor(name, shape, dtype, kind="ExternalInput").ap()

    x_in = I("x", [HID, T])
    qkvT = I("qkvT", [NQT, 128, 16 * 128])
    gT = I("gT", [NF, 128, 16 * 128])
    bTg = I("bT", [HID, H])
    gkTg = I("gkT", [HID, H])
    oTw = I("oT", [NF, 128, 16 * 128])
    WgT = I("WgT", [HID, 2])
    W0aT = I("W0aT", [NFF // 2, 128, 16 * 256])
    W0bT = I("W0bT", [NF, 128, 32 * 128])
    W1aT = I("W1aT", [NFF // 2, 128, 16 * 256])
    W1bT = I("W1bT", [NF, 128, 32 * 128])
    convw = I("convw", [QKVD, 4])
    b0a_c = I("b0a_c", [128, NFF])
    b1a_c = I("b1a_c", [128, NFF])
    b0b_r = I("b0b_r", [1, HID])
    b1b_r = I("b1b_r", [1, HID])
    onw_b = I("onw_b", [128, 128])
    halfm = I("halfm", [128, 1])

    out_d = nc.dram_tensor("out", [HID, T], F32, kind="ExternalOutput").ap()

    W0ab = nc.dram_tensor("W0ab", [NFF // 2, 128, 16 * 256], BF16)
    W0bb = nc.dram_tensor("W0bb", [NF, 128, 32 * 128], BF16)
    W1ab = nc.dram_tensor("W1ab", [NFF // 2, 128, 16 * 256], BF16)
    W1bb = nc.dram_tensor("W1bb", [NF, 128, 32 * 128], BF16)
    qkvb = nc.dram_tensor("qkvb", [NQT, 128, 16 * 128], BF16)
    cc1_in = nc.dram_tensor("cc1_in", [128, NQT * 3], F32)
    cc1_out = nc.dram_tensor("cc1_out", [256, NQT * 3], F32)
    cc2_in = nc.dram_tensor("cc2_in", [128, H * DK], F32)
    cc2_out = nc.dram_tensor("cc2_out", [256, H * DK], F32)
    RG = [[0, 1], [2, 3], [4, 5], [6, 7]]

    with ExitStack() as stk:
        S = stk.enter_context

        A1C = 37 * 1024
        A2C = 24 * 1024
        resid = S(nc.sbuf_tensor("resid", [128, NF * T], F32))
        arena1 = S(nc.sbuf_tensor("arena1", [128, A1C], BF16))
        arena2 = S(nc.sbuf_tensor("arena2", [128, A2C], BF16))
        identb = S(nc.sbuf_tensor("identb", [128, 128], BF16))
        identf = S(nc.sbuf_tensor("identf", [128, 128], F32))
        ones1b = S(nc.sbuf_tensor("ones1b", [1, 128], BF16))
        onescolb = S(nc.sbuf_tensor("onescolb", [128, 1], BF16))
        trilS = S(nc.sbuf_tensor("trilS", [128, 128], F32))
        triuI4 = S(nc.sbuf_tensor("triuI4", [128, 512], BF16))
        cst = S(nc.sbuf_tensor("cst", [128, 12], F32))
        convw_s = S(nc.sbuf_tensor("convw_s", [128, NQT * 4], F32))
        b0a_s = S(nc.sbuf_tensor("b0a_s", [128, NFF], F32))
        b1a_s = S(nc.sbuf_tensor("b1a_s", [128, NFF], F32))
        onesrow = S(nc.sbuf_tensor("onesrow", [1, TC], BF16))
        onw_s = S(nc.sbuf_tensor("onw_s", [128, 128], F32))
        halfm_s = S(nc.sbuf_tensor("halfm_s", [128, 1], F32))
        wgT_s = S(nc.sbuf_tensor("wgT_s", [128, NF * 2], BF16))
        wgdiff = S(nc.sbuf_tensor("wgdiff", [128, NF], BF16))
        bT_s = S(nc.sbuf_tensor("bT_s", [128, NF * H], BF16))
        gkT_s = S(nc.sbuf_tensor("gkT_s", [128, NF * H], BF16))
        rowA = S(nc.sbuf_tensor("rowA", [1, TC], F32))
        rowB = S(nc.sbuf_tensor("rowB", [1, TC], F32))
        rowC = S(nc.sbuf_tensor("rowC", [1, TC], BF16))
        rowD = S(nc.sbuf_tensor("rowD", [1, TC], BF16))
        spfm = S(nc.sbuf_tensor("spfm", [16, T], BF16))
        befm = S(nc.sbuf_tensor("befm", [16, T], BF16))
        a_col = S(nc.sbuf_tensor("a_col", [128, NCH * 16], F32))
        r_col = S(nc.sbuf_tensor("r_col", [128, NCH * 16], F32))
        be_col = S(nc.sbuf_tensor("be_col", [128, NCH * 16], F32))
        aC_col = S(nc.sbuf_tensor("aC_col", [128, NCH * 16], F32))
        rms_s = S(nc.sbuf_tensor("rms_s", [128, 16], F32))
        scr16 = S(nc.sbuf_tensor("scr16", [16, TC], F32))
        triuN16 = S(nc.sbuf_tensor("triuN16", [128, 128], BF16))
        stage3 = S(nc.sbuf_tensor("stage3", [128, NQT * 3], F32))

        PS = [S(nc.psum_tensor(f"ps{i}", [128, 512], F32)) for i in range(8)]

        sems = Sems(stk, nc, ["w", "xd", "mm", "ev", "ve", "cc", "st", "gc", "wc", "mmB", "evB", "veB"])

        def A1(off_kb2, ncols, dt=None):
            ap = arena1[:, off_kb2:off_kb2 + ncols]
            return ap if dt is None else ap.bitcast(dt)

        def A2(off, ncols, dt=None):
            ap = arena2[:, off:off + ncols]
            return ap if dt is None else ap.bitcast(dt)

        wstg = [None, None]

        with nc.Block(no_gpsimd_drain=True) as blk:
            @blk.sync
            def _(e):
                e.dma_start(out=resid[:, :].rearrange("p (f t) -> p f t", f=NF),
                            in_=x_in.rearrange("(f p) t -> p f t", p=128)).then_inc(*sems.inc("xd", 16))
                e.dma_start(out=b0a_s[:, :], in_=b0a_c[:, :]).then_inc(*sems.inc("xd", 16))
                e.dma_start(out=b1a_s[:, :], in_=b1a_c[:, :]).then_inc(*sems.inc("xd", 16))

                e.dma_start(out=onw_s[:, :], in_=onw_b[:, :]).then_inc(*sems.inc("xd", 16))
                e.dma_start(out=halfm_s[:, :], in_=halfm[:, :]).then_inc(*sems.inc("xd", 16))
                e.dma_start(out=convw_s[:, :].rearrange("p (t c) -> p t c", c=4),
                            in_=convw.rearrange("(t p) c -> p t c", p=128)).then_inc(*sems.inc("xd", 16))

            @blk.gpsimd
            def _(g):
                g.dma_start(out=wgT_s[:, :].rearrange("p (k c) -> p k c", c=2),
                            in_=WgT.rearrange("(k p) c -> p k c", p=128)).then_inc(*sems.inc("w", 16))
                g.dma_start(out=bT_s[:, :].rearrange("p (k c) -> p k c", c=H),
                            in_=bTg.rearrange("(k p) c -> p k c", p=128)).then_inc(*sems.inc("w", 16))
                g.dma_start(out=gkT_s[:, :].rearrange("p (k c) -> p k c", c=H),
                            in_=gkTg.rearrange("(k p) c -> p k c", p=128)).then_inc(*sems.inc("w", 16))
                g.iota(identf[:, :], [[-1, 128]], base=0, channel_multiplier=1,
                       allow_small_or_imprecise_dtypes=True)
                g.tensor_scalar(trilS[:, :], identf[:, :], 0.0, None, ALU.is_gt)
                g.iota(triuI4[:, :], [[0, 4], [-1, 128]], base=0, channel_multiplier=1,
                       allow_small_or_imprecise_dtypes=True)
                g.tensor_scalar(triuI4[:, :], triuI4[:, :], 0.0, None, ALU.is_le)
                g.tensor_scalar(identb[:, :], identf[:, :], 0.0, None, ALU.is_equal)
                g.tensor_scalar(identf[:, :], identf[:, :], 0.0, None, ALU.is_equal)
                g.tensor_scalar(triuN16[:, :], triuI4[:, 0:128], -1.0 / 16.0, None,
                                ALU.mult)
                g.memset(ones1b[:, :], 1.0)
                g.memset(onescolb[:, :], 1.0)
                for i, v in enumerate(CONSTS):
                    g.memset(cst[:, i:i + 1], v)
                g.memset(onesrow[:, :], 1.0)
                g.wait_ge(sems["xd"], sems.val("xd"))
                g.wait_ge(sems["w"], sems.val("w"))
                wg3 = wgT_s[:, :].rearrange("p (k c) -> p k c", c=2)
                g.tensor_tensor(wgdiff[:, :].rearrange("p (k o) -> p k o", o=1),
                                wg3[:, :, 0:1], wg3[:, :, 1:2], ALU.subtract)

        for i, v in enumerate(CONSTS):
            if v not in (0.0, 1.0):
                nc.const_aps.register(F32, v, cst[:, i:i + 1])
        nc.const_aps.register(F32, 1.0, cst[:, 0:1])

        def emit_ln_chunk(tc0, xn_ap):
            return emit_ln_chunk_to(tc0, xn_ap, TC, dst0=0)

        def emit_ln_chunk_to(tc0, xn_ap, fstride, dst0=None):
            if dst0 is None:
                dst0 = tc0
            XSL = [xn_ap[:, f * fstride + dst0: f * fstride + dst0 + TC]
                   for f in range(NF)]
            with nc.Block(no_gpsimd_drain=True) as b:
                @b.scalar
                def _(e):
                    for f in range(NF):
                        ins = e.activation(XSL[f],
                                           resid[:, f * T + tc0:f * T + tc0 + TC], AF.Copy)
                    ins.then_inc(*sems.inc("ev"))

                @b.tensor
                def _(t):
                    t.wait_ge(sems["ev"], sems.val("ev"))
                    for f in range(NF):
                        ins = t.matmul(PS[4][0:1, 0:TC], onescolb[:, :],
                                       XSL[f],
                                       start=(f == 0), stop=(f == NF - 1))
                    ins.then_inc(*sems.inc("mm"))

                @b.vector
                def _(v):
                    v.wait_ge(sems["mm"], sems.val("mm"))
                    for f in range(NF):
                        ins = v.tensor_tensor(XSL[f], XSL[f], XSL[f], ALU.mult)
                    ins.then_inc(*sems.inc("ve"))

                @b.tensor
                def _(t):
                    t.wait_ge(sems["ve"], sems.val("ve"))
                    for f in range(NF):
                        ins = t.matmul(PS[5][0:1, 0:TC], onescolb[:, :],
                                       XSL[f],
                                       start=(f == 0), stop=(f == NF - 1))
                    ins.then_inc(*sems.inc("mm"))

                @b.vector
                def _(v):
                    v.wait_ge(sems["mm"], sems.val("mm"))
                    v.tensor_scalar(rowA[:, :], PS[4][0:1, 0:TC], 1.0 / HID, None, ALU.mult)
                    v.tensor_tensor(rowB[:, :], rowA[:, :], rowA[:, :], ALU.mult)
                    v.scalar_tensor_tensor(rowB[:, :], PS[5][0:1, 0:TC], 1.0 / HID,
                                           rowB[:, :], ALU.mult, ALU.subtract)
                    v.tensor_copy(rowC[:, :], rowA[:, :]).then_inc(*sems.inc("ve"))

                @b.vector
                def _(v):
                    v.wait_ge(sems["ve"], sems.val("ve"))
                    v.tensor_scalar(rowB[:, :], rowB[:, :], EPS_LN, None, ALU.add)
                    v.reciprocal(rowB[:, :], rowB[:, :]).then_inc(*sems.inc("ve"))

                @b.scalar
                def _(e):
                    e.wait_ge(sems["ve"], sems.val("ve"))
                    e.activation(rowD[:, :], rowB[:, :], AF.Sqrt).then_inc(*sems.inc("ev"))

                @b.tensor
                def _(t):
                    t.wait_ge(sems["ev"], sems.val("ev"))
                    t.matmul(PS[6][:, 0:TC], ones1b[:, :], rowC[:, :], start=True, stop=True)
                    t.matmul(PS[7][:, 0:TC], ones1b[:, :], rowD[:, :],
                             start=True, stop=True).then_inc(*sems.inc("mm"))

                @b.vector
                def _(v):
                    v.wait_ge(sems["mm"], sems.val("mm"))
                    for f in range(NF):
                        sl = XSL[f]
                        v.tensor_tensor(sl, resid[:, f * T + tc0:f * T + tc0 + TC],
                                        PS[6][:, 0:TC], ALU.subtract)
                        ins = v.tensor_tensor(sl, sl, PS[7][:, 0:TC], ALU.mult)
                    ins.then_inc(*sems.inc("ve"))

        def stream_matmul(wT_ap, kdim, mtiles, rhs_ap, ncols, wsl, mg, nsplit,
                          evac_fn, bias_row=None, rstride=None, roffs=(0,),
                          evac_engine="scalar", wmode="cast", wstore=None,
                          carry_in=False, prefetch_ap=None, rhs_evs=None,
                          blk=None, bank_base=0, entry_ve=None, evac_guard_mm=None):
            ktiles = kdim // 128
            sc = mg * 128
            nst = mtiles // mg
            nr = len(roffs)
            w0 = sems.val("w")
            mm0 = sems.val("mm")
            ev0 = sems.val("ev")
            wcq0 = sems.val("wc")
            nq = (wsl[0].shape[1] // 1024) if wmode == "castv" else 0
            w_after = [w0 + (s + 1) * (nq if nq else 1) * 16 for s in range(nst)]
            if carry_in:
                w_after = [w0 + max(0, s - 1) * 16 for s in range(nst)]
            mm_after = [mm0 + (s + 1) for s in range(nst)]
            ev_after = [ev0 + (s + 1) for s in range(nst)]
            sems.c["w"] = w_after[-1] if not carry_in else w0 + (nst - 2) * 16
            sems.c["mm"] = mm_after[-1]
            sems.c["ev"] = ev_after[-1]
            if wmode == "castv":
                sems.c["wc"] = wcq0 + nst * nq
            ret_ev = list(ev_after)
            rstride_ = ncols if rstride is None else rstride
            wb0 = sems.val("st")
            if wstore is not None:
                sems.c["st"] = wb0 + nst * 16

            from contextlib import nullcontext
            with (nc.Block(no_gpsimd_drain=True) if blk is None else nullcontext(blk)) as b:
                nslot = len(wsl)

                def _load_body(g):
                    for s2 in range(nst):
                        if carry_in and s2 < 2:
                            continue
                        if s2 >= nslot:
                            g.wait_ge(sems["mm"], mm_after[s2 - nslot])
                            if wstore is not None:
                                g.wait_ge(sems["st"], wb0 + (s2 - 1) * 16)
                        slot = wsl[s2 % nslot]
                        g.dma_start(out=slot[:, :],
                                    in_=wT_ap[s2]).then_inc(sems["w"], 16)
                    if prefetch_ap is not None:
                        if wstore is not None:
                            g.wait_ge(sems["st"], sems.val("st"))
                        def _last_use(k):
                            s = nst - 1
                            while s % nslot != k:
                                s -= 1
                            return s
                        order = sorted((0, 1), key=_last_use)
                        for k in order:
                            g.wait_ge(sems["mm"], mm_after[_last_use(k)])
                            g.dma_start(out=wsl[k][:, :],
                                        in_=prefetch_ap[k]).then_inc(sems["w"], 16)
                        sems.c["w"] += 32
                if wmode == "cast":
                    b.gpsimd(_load_body)
                elif wmode == "bf16":
                    b.sync(_load_body)
                else:
                    @b.sync
                    def _(e):
                        qi = 0
                        for s2 in range(nst):
                            if s2 >= 2:
                                e.wait_ge(sems["mm"], mm_after[s2 - 2])
                                if wstore is not None:
                                    e.wait_ge(sems["st"], wb0 + (s2 - 1) * 16)
                            for q in range(nq):
                                if qi >= 2:
                                    e.wait_ge(sems["wc"], wcq0 + qi - 1)
                                e.dma_start(out=wstg[qi % 2],
                                            in_=wT_ap[s2][:, q * 1024:(q + 1) * 1024]
                                            ).then_inc(sems["w"], 16)
                                qi += 1

                    @b.vector
                    def _(v):
                        qi = 0
                        for s2 in range(nst):
                            slot = wsl[s2 % nslot]
                            for q in range(nq):
                                v.wait_ge(sems["w"], w0 + (qi + 1) * 16)
                                v.tensor_copy(slot[:, q * 1024:(q + 1) * 1024],
                                              wstg[qi % 2]).then_inc(sems["wc"], 1)
                                qi += 1

                if wstore is not None:
                    def _store_body(e):
                        for s2 in range(nst):
                            e.wait_ge(sems["mm"], mm_after[s2])
                            e.dma_start(out=wstore[s2],
                                        in_=wsl[s2 % nslot][:, :]).then_inc(sems["st"], 16)
                    if wmode == "castv":
                        b.scalar(_store_body)
                    else:
                        b.sync(_store_body)

                @b.tensor
                def _(t):
                    for s2 in range(nst):
                        if wmode == "castv":
                            t.wait_ge(sems["wc"], wcq0 + (s2 + 1) * nq)
                        else:
                            t.wait_ge(sems["w"], w_after[s2])
                        if s2 >= 2:
                            t.wait_ge(sems["ev"], ev_after[s2 - 2])
                        if rhs_evs is not None and s2 == 1:
                            t.wait_ge(sems["ev"], rhs_evs[-1])
                        if s2 == 0 and entry_ve is not None:
                            t.wait_ge(sems["ve"], entry_ve)
                        if s2 == 0 and rhs_evs is None and entry_ve is None and blk is not None:
                            t.wait_ge(sems["ev"], ev0)
                        slot = wsl[s2 % nslot]
                        bi = bank_base + (0 if s2 % 2 == 0 else 2)
                        for m in range(mg):
                            mi = s2 * mg + m
                            for ci in range(nr):
                                bank = PS[bi + m * nr + ci]
                                for k in range(ktiles):
                                    if rhs_evs is not None and s2 == 0 and m == 0 and ci == 0:
                                        t.wait_ge(sems["ev"], rhs_evs[min(k, len(rhs_evs) - 1)])
                                    ins = t.matmul(bank[:, 0:ncols],
                                                   slot[:, k * sc + m * 128:
                                                        k * sc + (m + 1) * 128],
                                                   rhs_ap[:, k * rstride_ + roffs[ci]:
                                                          k * rstride_ + roffs[ci] + ncols],
                                                   start=(k == 0),
                                                   stop=(k == ktiles - 1 and bias_row is None))
                                if bias_row is not None:
                                    ins = t.matmul(bank[:, 0:ncols],
                                                   bias_row[:, mi * 128:(mi + 1) * 128],
                                                   onesrow[:, 0:ncols],
                                                   start=False, stop=True)
                        ins.then_inc(sems["mm"], 1)

                def _evac_body(e):
                    for s2 in range(nst):
                        if s2 == 0 and evac_guard_mm is not None:
                            e.wait_ge(sems["mm"], evac_guard_mm)
                        e.wait_ge(sems["mm"], mm_after[s2])
                        bi = bank_base + (0 if s2 % 2 == 0 else 2)
                        for m in range(mg):
                            for ci in range(nr):
                                bank = PS[bi + m * nr + ci]
                                ins = evac_fn(e, s2 * mg + m, bank[:, 0:ncols], ci)
                        ins.then_inc(sems["ev"], 1)
                if evac_engine == "scalar":
                    b.scalar(_evac_body)
                else:
                    b.vector(_evac_body)
            return ret_ev

        def emit_ffn_sublayer(first):
            xn = A1(0, NF * TC)
            hbuf = A1(8192, NFF * TC)
            wsl = (A1(24576, 4096), A1(28672, 4096), A1(32768, 4096))
            wstg[0] = A1(36864, 512, F32)
            wstg[1] = A1(36864, 512, F32)
            out0 = A2(0, NF * TC)
            out1 = A2(8192, NF * TC)
            msku = A2(16384, 256, U8)
            b0br_s = arena2[0:1, 16896:16896 + HID]
            b1br_s = arena2[0:1, 19072:19072 + HID]
            with nc.Block(no_gpsimd_drain=True) as b:
                @b.gpsimd
                def _(g):
                    g.dma_start(out=b0br_s, in_=b0b_r[:, :]).then_inc(*sems.inc("w", 16))
                    g.dma_start(out=b1br_s, in_=b1b_r[:, :]).then_inc(*sems.inc("w", 16))
                    g.wait_ge(sems["w"], sems.val("w"))
            for tci, tc0 in enumerate(range(0, T, TC)):
                cast_pass = False
                wm = "cast"
                W0aS, W0bS, W1aS, W1bS = W0aT, W0bT, W1aT, W1bT
                emit_ln_chunk(tc0, xn)
                with nc.Block(no_gpsimd_drain=True) as blkc:
                    @blkc.tensor
                    def _(t):
                        for k in range(NF):
                            ins = t.matmul(PS[4][0:1, 0:TC], wgdiff[:, k:k + 1],
                                           xn[:, k * TC:(k + 1) * TC],
                                           start=(k == 0), stop=(k == NF - 1))
                        ins.then_inc(*sems.inc("mm"))

                    @blkc.vector
                    def _(v):
                        v.wait_ge(sems["mm"], sems.val("mm"))
                        v.tensor_copy(rowC[:, :], PS[4][0:1, 0:TC]).then_inc(*sems.inc("ve"))

                    @blkc.tensor
                    def _(t):
                        t.wait_ge(sems["ve"], sems.val("ve"))
                        t.matmul(PS[5][:, 0:TC], ones1b[:, :], rowC[:, :],
                                 start=True, stop=True).then_inc(*sems.inc("mm"))

                    @blkc.vector
                    def _(v):
                        v.wait_ge(sems["mm"], sems.val("mm"))
                        v.tensor_scalar(msku[:, 0:TC], PS[5][:, 0:TC], 0.0, None,
                                        ALU.is_ge).then_inc(*sems.inc("ve"))

                    ve_gate = sems.val("ve")

                    def evac_h0(e, mi, ps, ci):
                        return e.activation(hbuf[:, mi * TC:(mi + 1) * TC], ps, AF.Gelu,
                                            bias=b0a_s[:, mi:mi + 1])

                    ev_h = stream_matmul(W0aS, HID, NFF, xn, TC, wsl, 2, 2, evac_h0,
                                         wmode=wm, carry_in=(tci > 0),
                                         prefetch_ap=W0bS, blk=blkc, bank_base=0)
                    ev_hk = [ev_h[k // 2] for k in range(NFF)]
                    mm_a = sems.val("mm")

                    def evac_o0(e, mi, ps, ci):
                        return e.activation(out0[:, mi * TC:(mi + 1) * TC], ps, AF.Copy,
                                            scale=0.5)

                    stream_matmul(W0bS, FFD, NF, hbuf, TC, wsl, 1, 4, evac_o0,
                                  bias_row=b0br_s, wmode=wm, carry_in=True,
                                  prefetch_ap=W1aS, rhs_evs=ev_hk, blk=blkc,
                                  bank_base=4, entry_ve=ve_gate)
                    mm_b0 = sems.val("mm")

                    def evac_h1(e, mi, ps, ci):
                        return e.activation(hbuf[:, mi * TC:(mi + 1) * TC], ps, AF.Relu,
                                            bias=b1a_s[:, mi:mi + 1])

                    ev_h = stream_matmul(W1aS, HID, NFF, xn, TC, wsl, 2, 2, evac_h1,
                                         wmode=wm, carry_in=True, prefetch_ap=W1bS,
                                         blk=blkc, bank_base=0,
                                         evac_guard_mm=mm_b0)
                    ev_hk = [ev_h[k // 2] for k in range(NFF)]

                    def evac_o1(e, mi, ps, ci):
                        return e.activation(out1[:, mi * TC:(mi + 1) * TC], ps, AF.Copy,
                                            scale=0.5)

                    stream_matmul(W1bS, FFD, NF, hbuf, TC, wsl, 1, 4, evac_o1,
                                  bias_row=b1br_s, wmode=wm, carry_in=True,
                                  rhs_evs=ev_hk, blk=blkc, bank_base=4,
                                  prefetch_ap=W0aT if tci == 0 else None)

                    ev_fin = sems.val("ev")

                    @blkc.vector
                    def _(v):
                        v.wait_ge(sems["ev"], ev_fin)
                        for f in range(NF):
                            sl = hbuf[:, f * TC:(f + 1) * TC]
                            v.select(sl, msku[:, 0:TC], out0[:, f * TC:(f + 1) * TC],
                                     out1[:, f * TC:(f + 1) * TC])
                            ins = v.tensor_tensor(resid[:, f * T + tc0:f * T + tc0 + TC],
                                                  resid[:, f * T + tc0:f * T + tc0 + TC],
                                                  sl, ALU.add)
                        ins.then_inc(*sems.inc("ve"))

        def a1ap(off, dims):
            return bass.AP(arena1[:, :].tensor, off, [[A1C, 128]] + dims)

        def a2ap(off, dims):
            return bass.AP(arena2[:, :].tensor, off, [[A2C, 128]] + dims)

        KOFF = 16 * T
        VOFF = 20 * T

        def qslice(h, j):
            return arena2[:, h * T + j * C: h * T + j * C + C]

        def kslice(kv, j):
            return arena2[:, KOFF + kv * T + j * C: KOFF + kv * T + j * C + C]

        def vslice(kv, j):
            return arena2[:, VOFF + kv * T + j * C: VOFF + kv * T + j * C + C]

        RAWOFF = 8192

        def rawt(q):
            return A1(RAWOFF + q * 1027, 1027)

        def emit_p2():
            xn = A1(0, NF * TC)
            wslq = (A1(32840, 2048), A1(34888, 2048))
            for tci, tc0 in enumerate(range(0, T, TC)):
                emit_ln_chunk(tc0, xn)

                def evac_qkv(e, mi, ps, ci, tc0=tc0):
                    return e.activation(rawt(mi)[:, 3 + tc0:3 + tc0 + TC], ps, AF.Copy)

                stream_matmul(qkvT, HID, NQT, xn, TC, wslq, 1, 2, evac_qkv)

                with nc.Block(no_gpsimd_drain=True) as b:
                    @b.tensor
                    def _(t):
                        for k in range(NF):
                            ins = t.matmul(PS[4][0:16, 0:TC], bT_s[:, k * H:(k + 1) * H],
                                           xn[:, k * TC:(k + 1) * TC],
                                           start=(k == 0), stop=(k == NF - 1))
                        for k in range(NF):
                            ins = t.matmul(PS[5][0:16, 0:TC], gkT_s[:, k * H:(k + 1) * H],
                                           xn[:, k * TC:(k + 1) * TC],
                                           start=(k == 0), stop=(k == NF - 1))
                        ins.then_inc(*sems.inc("mm"))

                    @b.scalar
                    def _(e, tc0=tc0):
                        e.wait_ge(sems["mm"], sems.val("mm"))
                        e.activation(befm[:, tc0:tc0 + TC], PS[4][0:16, 0:TC], AF.Sigmoid)
                        e.activation(scr16[:, 0:TC], PS[5][0:16, 0:TC], AF.Exp, scale=-1.0)
                        e.activation(spfm[:, tc0:tc0 + TC], scr16[:, 0:TC], AF.Ln,
                                     bias=1.0).then_inc(*sems.inc("ev"))

            with nc.Block(no_gpsimd_drain=True) as b:
                @b.gpsimd
                def _(g):
                    g.dma_start(out=cc1_in.ap().rearrange("p (q c) -> p q c", c=3),
                                in_=a1ap(RAWOFF + 1024, [[1027, 24], [1, 3]])).then_inc(*sems.inc("xd", 16))
                    g.wait_ge(sems["xd"], sems.val("xd"))
                    g.collective_compute("AllGather", ALU.bypass, replica_groups=RG,
                                         ins=[cc1_in.ap().opt()],
                                         outs=[cc1_out.ap().opt()]).then_inc(*sems.inc("cc"))

            with nc.Block(no_gpsimd_drain=True) as b:
                cscr = [A1(32840, 2048, F32), A1(34888, 2048, F32)]
                ve0 = sems.val("ve")
                ev0 = sems.val("ev")
                gc0 = sems.val("gc")

                @b.vector
                def _(v):
                    for q in range(NQT):
                        if q >= 2:
                            v.wait_ge(sems["ev"], ev0 + q - 1)
                        cs = cscr[q % 2][:, 0:1021]
                        rt = rawt(q)
                        v.tensor_scalar(cs, rt[:, 6:1027], convw_s[:, q * 4 + 3:q * 4 + 4],
                                        None, ALU.mult)
                        for i in range(3):
                            sh = 3 - i
                            ins = v.scalar_tensor_tensor(cs, rt[:, 6 - sh:1027 - sh],
                                                         convw_s[:, q * 4 + i:q * 4 + i + 1],
                                                         cs, ALU.mult, ALU.add)
                        ins.then_inc(sems["ve"], 1)
                sems.c["ve"] = ve0 + NQT

                @b.scalar
                def _(e):
                    for q in range(NQT):
                        e.wait_ge(sems["ve"], ve0 + q + 1)
                        e.activation(arena2[:, q * T + 3:(q + 1) * T],
                                     cscr[q % 2][:, 0:1021],
                                     AF.Silu).then_inc(sems["ev"], 1)
                sems.c["ev"] = ev0 + NQT

            with nc.Block(no_gpsimd_drain=True) as b:
                @b.gpsimd
                def _(g):
                    g.wait_ge(sems["cc"], sems.val("cc"))
                    g.dma_start(out=a1ap(RAWOFF, [[1027, 24], [1, 3]]),
                                in_=cc1_out.ap()[0:128, :].rearrange("p (q c) -> p q c", c=3)).then_inc(*sems.inc("xd", 16))
                    g.wait_ge(sems["xd"], sems.val("xd"))

                @b.vector
                def _(v):
                    v.wait_ge(sems["xd"], sems.val("xd"))
                    v.tensor_scalar(a1ap(RAWOFF, [[1027, 24], [1, 3]]),
                                    a1ap(RAWOFF, [[1027, 24], [1, 3]]),
                                    halfm_s[:, 0:1], None, ALU.mult)
                    cs = cscr[0]
                    for q in range(NQT):
                        rt = rawt(q)
                        v.tensor_scalar(cs[:, q * 4:q * 4 + 3], rt[:, 3:6],
                                        convw_s[:, q * 4 + 3:q * 4 + 4], None, ALU.mult)
                        for i in range(3):
                            sh = 3 - i
                            ins = v.scalar_tensor_tensor(cs[:, q * 4:q * 4 + 3],
                                                         rt[:, 3 - sh:6 - sh],
                                                         convw_s[:, q * 4 + i:q * 4 + i + 1],
                                                         cs[:, q * 4:q * 4 + 3],
                                                         ALU.mult, ALU.add)
                    ins.then_inc(*sems.inc("ve"))

                @b.scalar
                def _(e):
                    e.wait_ge(sems["ve"], sems.val("ve"))
                    for q in range(NQT):
                        ins = e.activation(arena2[:, q * T:q * T + 3],
                                           cscr[0][:, q * 4:q * 4 + 3], AF.Silu)
                    ins.then_inc(*sems.inc("ev"))

            with nc.Block(no_gpsimd_drain=True) as b:
                qsq = [A1(32840, 512), A1(33352, 512)]
                rowBs = [rowB, A1(33864, 1024, F32)]
                rowDs = [rowD, A1(34888, 512)]
                LS = [("mm", "ev", "ve"), ("mmB", "evB", "veB")]
                iters = [(tl, tc) for tl in range(20) for tc in range(NTC)]

                def l2_steps(it, tl, tc):
                    L = it % 2
                    smm, sev, sve = LS[L]
                    sl = arena2[:, tl * T + tc * TC: tl * T + tc * TC + TC]
                    bank = PS[4 + L]
                    bank2 = PS[6 + L]
                    qb = qsq[L]
                    rB = rowBs[L]
                    rD = rowDs[L]

                    def sq(v):
                        v.tensor_tensor(qb[:, 0:TC], sl, sl, ALU.mult).then_inc(sems[sve], 1)
                        sems.c[sve] += 1

                    def summ(t):
                        t.wait_ge(sems[sve], sems.val(sve))
                        t.matmul(bank[0:1, 0:TC], onescolb[:, :], qb[:, 0:TC],
                                 start=True, stop=True).then_inc(sems[smm], 1)
                        sems.c[smm] += 1

                    def rowsa(v):
                        v.wait_ge(sems[smm], sems.val(smm))
                        v.tensor_scalar(rB[0:1, 0:TC], bank[0:1, 0:TC], EPS_L2, None, ALU.add)
                        v.reciprocal(rB[0:1, 0:TC], rB[0:1, 0:TC]).then_inc(sems[sve], 1)
                        sems.c[sve] += 1

                    def rows(e):
                        e.wait_ge(sems[sve], sems.val(sve))
                        if tl < 16:
                            e.activation(rD[0:1, 0:TC], rB[0:1, 0:TC], AF.Sqrt, scale=1.0 / 128.0)
                        else:
                            e.activation(rD[0:1, 0:TC], rB[0:1, 0:TC], AF.Sqrt)
                        e.drain().then_inc(sems[sev], 1)
                        sems.c[sev] += 1

                    def bcast(t):
                        t.wait_ge(sems[sev], sems.val(sev))
                        t.matmul(bank2[:, 0:TC], ones1b[:, :], rD[0:1, 0:TC],
                                 start=True, stop=True).then_inc(sems[smm], 1)
                        sems.c[smm] += 1

                    def apply(v):
                        v.wait_ge(sems[smm], sems.val(smm))
                        v.tensor_tensor(sl, sl, bank2[:, 0:TC], ALU.mult).then_inc(sems[sve], 1)
                        sems.c[sve] += 1

                    return [("vector", sq), ("tensor", summ), ("vector", rowsa),
                            ("scalar", rows), ("tensor", bcast), ("vector", apply)]

                for pi in range(0, len(iters), 2):
                    st0 = l2_steps(pi, *iters[pi])
                    st1 = l2_steps(pi + 1, *iters[pi + 1]) if pi + 1 < len(iters) else None
                    for si in range(6):
                        eng, fn = st0[si]
                        getattr(b, eng)(fn)
                        if st1 is not None:
                            eng, fn = st1[si]
                            getattr(b, eng)(fn)
        A4o, Vt4o, Kt4o, Ngo = 0, 512, 1024, 1536
        Mpo = 2048
        Nto = 5632
        Xo = 6144
        Gto = 10240
        Qeo = 12288
        Mto = 14336
        Tso = 16384
        Pho = 18432
        Info = 20480

        def Xh(h):
            return A1(Xo + h * 256, 256)

        def emit_p3_chunk(j):
            LS = [("mm", "ev", "ve"), ("mmB", "evB", "veB")]

            def lane_sched(g):
                L = g % 2
                smm, sev, sve = LS[L]
                bksq = PS[3 + L]
                bkap = PS[5 + L]
                base = 1536 + L * 2304
                Ngo_ = base
                Mpo_ = base + 256
                Nto_ = base + 2048
                h0 = 2 * g
                steps = []

                def nmask(v):
                    v.wait_ge(sems["ev"], kv_ev)
                    for q in range(2):
                        h = h0 + q
                        ins = v.scalar_tensor_tensor(
                            A1(Ngo_ + q * 128, 128), A1(A4o + (h // 4) * 128, 128),
                            be_col[:, j * 16 + h:j * 16 + h + 1], trilS[:, :],
                            ALU.mult, ALU.mult)
                    ins.then_inc(sems[sve], 1)
                    sems.c[sve] += 1
                steps.append(("vector", nmask))

                def m1(t):
                    t.wait_ge(sems[sve], sems.val(sve))
                    for q in range(2):
                        ins = t.matmul(bksq[:, q * 128:(q + 1) * 128],
                                       A1(Ngo_ + q * 128, 128), identb[:, :],
                                       start=True, stop=True)
                    ins.then_inc(sems[smm], 1)
                    sems.c[smm] += 1
                steps.append(("tensor", m1))

                def m1e(v):
                    v.wait_ge(sems[smm], sems.val(smm))
                    v.tensor_copy(A1(Mpo_, 256),
                                  bksq[:, 0:256]).then_inc(sems[sve], 1)
                    sems.c[sve] += 1
                steps.append(("vector", m1e))

                for lv in range(1, 7):
                    def sq(t, lv=lv):
                        t.wait_ge(sems[sve], sems.val(sve))
                        for q in range(2):
                            srcn = A1(Ngo_ + q * 128, 128) if lv == 1 else A1(Nto_ + q * 128, 128)
                            ins = t.matmul(bksq[:, q * 128:(q + 1) * 128], srcn,
                                           A1(Mpo_ + (lv - 1) * 256 + q * 128, 128),
                                           start=True, stop=True)
                        ins.then_inc(sems[smm], 1)
                        sems.c[smm] += 1
                    steps.append(("tensor", sq))

                    def sqe(v, lv=lv):
                        v.wait_ge(sems[smm], sems.val(smm))
                        v.tensor_copy(A1(Mpo_ + lv * 256, 256),
                                      bksq[:, 0:256]).then_inc(sems[sve], 1)
                        sems.c[sve] += 1
                    steps.append(("vector", sqe))

                    if lv < 6:
                        def tr(t, lv=lv):
                            t.wait_ge(sems[sve], sems.val(sve))
                            for q in range(2):
                                ins = t.matmul(bksq[:, 256 + q * 128:256 + (q + 1) * 128],
                                               A1(Mpo_ + lv * 256 + q * 128, 128),
                                               identb[:, :], start=True, stop=True)
                            ins.then_inc(sems[smm], 1)
                            sems.c[smm] += 1
                        steps.append(("tensor", tr))

                        def tre(v):
                            v.wait_ge(sems[smm], sems.val(smm))
                            v.tensor_copy(A1(Nto_, 256),
                                          bksq[:, 256:512]).then_inc(sems[sve], 1)
                            sems.c[sve] += 1
                        steps.append(("vector", tre))

                def xinit(v):
                    v.wait_ge(sems[sve], sems.val(sve))
                    for q in range(2):
                        h = h0 + q
                        kv = h // 4
                        v.tensor_scalar(Xh(h)[:, 0:128], A1(Vt4o + kv * 128, 128),
                                        r_col[:, j * 16 + h:j * 16 + h + 1], None, ALU.mult)
                        ins = v.tensor_scalar(Xh(h)[:, 128:256], A1(Kt4o + kv * 128, 128),
                                              be_col[:, j * 16 + h:j * 16 + h + 1],
                                              None, ALU.mult)
                    ins.then_inc(sems[sve], 1)
                    sems.c[sve] += 1
                steps.append(("vector", xinit))

                for lv in range(6, 0, -1):
                    def ap(t, lv=lv):
                        t.wait_ge(sems[sve], sems.val(sve))
                        for q in range(2):
                            h = h0 + q
                            co = q * 256
                            ins = t.matmul(bkap[:, co:co + 256],
                                           A1(Mpo_ + lv * 256 + q * 128, 128), Xh(h),
                                           start=True, stop=True)
                        ins.then_inc(sems[smm], 1)
                        sems.c[smm] += 1
                    steps.append(("tensor", ap))

                    def ape(v):
                        v.wait_ge(sems[smm], sems.val(smm))
                        v.scalar_tensor_tensor(A1(Xo + h0 * 256, 512), bkap[:, :],
                                               1.0, A1(Xo + h0 * 256, 512),
                                               ALU.mult, ALU.add).then_inc(sems[sve], 1)
                        sems.c[sve] += 1
                    steps.append(("vector", ape))

                def fin(t):
                    t.wait_ge(sems[sve], sems.val(sve))
                    for q in range(2):
                        h = h0 + q
                        ins = t.matmul(bkap[:, q * 256:(q + 1) * 256],
                                       A1(Mpo_ + q * 128, 128),
                                       Xh(h), start=True, stop=True)
                    ins.then_inc(sems[smm], 1)
                    sems.c[smm] += 1
                steps.append(("tensor", fin))

                def fine(v):
                    v.wait_ge(sems[smm], sems.val(smm))
                    v.tensor_tensor(A1(Xo + h0 * 256, 512), A1(Xo + h0 * 256, 512),
                                    bkap[:, :], ALU.subtract).then_inc(sems[sve], 1)
                    sems.c[sve] += 1
                steps.append(("vector", fine))

                def gt(t):
                    t.wait_ge(sems[sve], sems.val(sve))
                    for q in range(2):
                        h = h0 + q
                        ins = t.matmul(bkap[:, q * 128:(q + 1) * 128],
                                       kslice(h // 4, j), qslice(h, j),
                                       start=True, stop=True)
                    ins.then_inc(sems[smm], 1)
                    sems.c[smm] += 1
                steps.append(("tensor", gt))

                def gte(v, g=g):
                    v.wait_ge(sems[smm], sems.val(smm))
                    v.tensor_tensor(A1(Gto + h0 * 128, 256), bkap[:, 0:256],
                                    triuI4[:, 0:256], ALU.mult).then_inc(sems[sve], 1)
                    sems.c[sve] += 1
                steps.append(("vector", gte))

                def qe(t):
                    t.wait_ge(sems[sve], sems.val(sve))
                    for q in range(2):
                        h = h0 + q
                        ins = t.matmul(bkap[:, q * 128:(q + 1) * 128],
                                       Xh(h)[:, 128:256], A1(Gto + h * 128, 128),
                                       start=True, stop=True)
                    ins.then_inc(sems[smm], 1)
                    sems.c[smm] += 1
                steps.append(("tensor", qe))

                def qee(v):
                    v.wait_ge(sems[smm], sems.val(smm))
                    qv = a2ap(h0 * T + j * C, [[T, 2], [1, 128]])
                    v.tensor_tensor(A1(Qeo + h0 * 128, 256).rearrange(
                        "p (h c) -> p h c", c=128), qv,
                        bkap[:, 0:256].rearrange("p (h c) -> p h c", c=128),
                        ALU.subtract).then_inc(sems[sve], 1)
                    sems.c[sve] += 1
                steps.append(("vector", qee))

                def mt(t):
                    t.wait_ge(sems[sve], sems.val(sve))
                    for q in range(2):
                        h = h0 + q
                        kv = h // 4
                        ins = t.matmul(bkap[:, q * 128:(q + 1) * 128],
                                       Xh(h)[:, 128:256], A1(Kt4o + kv * 128, 128),
                                       start=True, stop=True)
                    ins.then_inc(sems[smm], 1)
                    sems.c[smm] += 1
                steps.append(("tensor", mt))

                def mte(v):
                    v.wait_ge(sems[smm], sems.val(smm))
                    idr = identf[:, :].unsqueeze(1).broadcast_to([128, 2, 128])
                    v.tensor_tensor(A1(Mto + h0 * 128, 256).rearrange(
                        "p (h c) -> p h c", c=128), idr,
                        bkap[:, 0:256].rearrange("p (h c) -> p h c", c=128),
                        ALU.subtract).then_inc(sems[sve], 1)
                    sems.c[sve] += 1
                steps.append(("vector", mte))

                return steps

            with nc.Block(no_gpsimd_drain=True) as b:
                @b.tensor
                def _(t):
                    for kv in range(4):
                        t.matmul(PS[0][:, kv * 128:(kv + 1) * 128], kslice(kv, j),
                                 kslice(kv, j), start=True, stop=True)
                        t.matmul(PS[1][:, kv * 128:(kv + 1) * 128], vslice(kv, j),
                                 identb[:, :], start=True, stop=True)
                        ins = t.matmul(PS[2][:, kv * 128:(kv + 1) * 128], kslice(kv, j),
                                       identb[:, :], start=True, stop=True)
                    ins.then_inc(*sems.inc("mm"))

                @b.scalar
                def _(e):
                    e.wait_ge(sems["mm"], sems.val("mm"))
                    e.activation(A1(A4o, 512), PS[0][:, :], AF.Copy)
                    e.activation(A1(Vt4o, 512), PS[1][:, :], AF.Copy)
                    ins = e.activation(A1(Kt4o, 512), PS[2][:, :], AF.Copy)
                    ins.then_inc(*sems.inc("ev"))

                kv_ev = sems.val("ev")

                sched0 = []
                sched1 = []
                for g in range(8):
                    (sched0 if g % 2 == 0 else sched1).append(g)
                q0 = [s for g in sched0 for s in lane_sched(g)] if False else None
                its0 = iter(sched0)
                its1 = iter(sched1)
                cur0 = lane_sched(next(its0))
                cur1 = lane_sched(next(its1))
                i0_, i1_ = 0, 0
                while cur0 is not None or cur1 is not None:
                    for lane in (0, 1):
                        if lane == 0 and cur0 is not None:
                            eng, fn = cur0[i0_]
                            getattr(b, eng)(fn)
                            i0_ += 1
                            if i0_ == len(cur0):
                                g_ = next(its0, None)
                                cur0 = lane_sched(g_) if g_ is not None else None
                                i0_ = 0
                        elif lane == 1 and cur1 is not None:
                            eng, fn = cur1[i1_]
                            getattr(b, eng)(fn)
                            i1_ += 1
                            if i1_ == len(cur1):
                                g_ = next(its1, None)
                                cur1 = lane_sched(g_) if g_ is not None else None
                                i1_ = 0

                for g in range(4):
                    h0 = 4 * g

                    def _ch(t, h0=h0):
                        t.wait_ge(sems["ve"], sems.val("ve"))
                        t.wait_ge(sems["veB"], sems.val("veB"))
                        t.wait_ge(sems["ev"], sems.val("ev"))
                        for q in range(4):
                            h = h0 + q
                            kv = h // 4
                            c0 = q * 128
                            t.matmul(PS[7][:, c0:c0 + 128], A1(Qeo + h * 128, 128),
                                     A1(Tso + h * 128, 128), start=True, stop=False)
                            t.matmul(PS[7][:, c0:c0 + 128], A1(Gto + h * 128, 128),
                                     Xh(h)[:, 0:128], start=False, stop=True)
                            t.matmul(PS[3][:, c0:c0 + 128], A1(Pho + h * 128, 128),
                                     A1(Qeo + h * 128, 128), start=True, stop=True)
                            t.matmul(PS[5][:, c0:c0 + 128], A1(Mto + h * 128, 128),
                                     A1(Tso + h * 128, 128), start=True, stop=False)
                            t.matmul(PS[5][:, c0:c0 + 128], A1(Kt4o + kv * 128, 128),
                                     Xh(h)[:, 0:128], start=False, stop=True)
                            ins = t.matmul(PS[6][:, c0:c0 + 128], A1(Mto + h * 128, 128),
                                           A1(Pho + h * 128, 128), start=True, stop=True)
                        ins.then_inc(sems["mm"], 1)
                    b.tensor(_ch)
                    sems.c["mm"] += 1

                    def _che(v, h0=h0, g=g):
                        v.wait_ge(sems["mm"], sems.val("mm"))
                        for q in range(4):
                            h = h0 + q
                            c0 = q * 128
                            v.tensor_scalar(qslice(h, j), PS[7][:, c0:c0 + 128],
                                            a_col[:, j * 16 + h:j * 16 + h + 1],
                                            None, ALU.mult)
                            v.tensor_scalar(A1(Tso + h * 128, 128), PS[5][:, c0:c0 + 128],
                                            aC_col[:, j * 16 + h:j * 16 + h + 1],
                                            None, ALU.mult)
                            ins = v.tensor_scalar(A1(Pho + h * 128, 128),
                                                  PS[6][:, c0:c0 + 128],
                                                  aC_col[:, j * 16 + h:j * 16 + h + 1],
                                                  None, ALU.mult)
                        ins.then_inc(sems["ve"], 1)
                    b.vector(_che)
                    sems.c["ve"] += 1

                    def _chei(e, g=g):
                        e.wait_ge(sems["mm"], sems.val("mm"))
                        ins = e.activation(A1(Info + j * 2048 + g * 512, 512), PS[3][:, :],
                                           AF.Copy)
                        ins.then_inc(sems["ev"], 1)
                    b.scalar(_chei)
                    sems.c["ev"] += 1

        def emit_p3():
            sptmb = A1(36864, 16)
            acolb = A1(36880, 16)
            rowacb = A1(36896, 16)
            rinvs = A1(36912, 32, F32)
            with nc.Block(no_gpsimd_drain=True) as b:
                @b.vector
                def _(v):
                    v.memset(A1(Tso, 2048), 0.0)
                    v.tensor_copy(A1(Pho, 2048).rearrange("p (h c) -> p h c", c=128),
                                  identb[:, :].unsqueeze(1).broadcast_to([128, 16, 128]))
                    v.nop().then_inc(*sems.inc("ve"))

                for j in range(NCH):
                    bk = PS[4 + (j % 2)]
                    bk2 = PS[6 + (j % 2)]

                    def _v1(t, j=j, bk=bk):
                        t.wait_ge(sems["ve"], sems.val("ve"))
                        t.matmul(bk[0:128, 0:16], spfm[:, j * C:(j + 1) * C],
                                 identb[0:16, 0:16], start=True, stop=True)
                        t.matmul(bk[0:128, 16:32], befm[:, j * C:(j + 1) * C],
                                 identb[0:16, 0:16], start=True, stop=True).then_inc(sems["mm"], 1)
                    b.tensor(_v1)
                    sems.c["mm"] += 1

                    def _e1(v, j=j, bk=bk):
                        v.wait_ge(sems["mm"], sems.val("mm"))
                        v.tensor_copy(sptmb, bk[0:128, 0:16])
                        v.tensor_copy(be_col[:, j * 16:(j + 1) * 16],
                                      bk[0:128, 16:32]).then_inc(sems["ve"], 1)
                    b.vector(_e1)
                    sems.c["ve"] += 1

                    def _t2(t, bk2=bk2):
                        t.wait_ge(sems["ve"], sems.val("ve"))
                        t.matmul(bk2[0:128, 0:16], triuN16[:, :], sptmb,
                                 start=True, stop=True).then_inc(sems["mm"], 1)
                    b.tensor(_t2)
                    sems.c["mm"] += 1

                    def _e2(e, j=j, bk2=bk2):
                        e.wait_ge(sems["mm"], sems.val("mm"))
                        e.activation(a_col[:, j * 16:(j + 1) * 16], bk2[0:128, 0:16], AF.Exp)
                        e.activation(acolb, bk2[0:128, 0:16], AF.Exp)
                        e.activation(rinvs[:, 0:16], bk2[0:128, 0:16], AF.Exp,
                                     scale=-1.0).then_inc(sems["ev"], 1)
                    b.scalar(_e2)
                    sems.c["ev"] += 1

                    def _v2(v, j=j):
                        v.wait_ge(sems["ev"], sems.val("ev"))
                        v.tensor_tensor(r_col[:, j * 16:(j + 1) * 16],
                                        be_col[:, j * 16:(j + 1) * 16],
                                        rinvs[:, 0:16], ALU.mult).then_inc(sems["ve"], 1)
                    b.vector(_v2)
                    sems.c["ve"] += 1

                    def _t3(t, bk=bk):
                        t.wait_ge(sems["ve"], sems.val("ve"))
                        t.matmul(bk[0:1, 32:48], identb[:, 127:128], acolb,
                                 start=True, stop=True).then_inc(sems["mm"], 1)
                    b.tensor(_t3)
                    sems.c["mm"] += 1

                    def _e3(v, bk=bk):
                        v.wait_ge(sems["mm"], sems.val("mm"))
                        v.tensor_copy(rowacb[0:1, :], bk[0:1, 32:48]).then_inc(sems["ve"], 1)
                    b.vector(_e3)
                    sems.c["ve"] += 1

                    def _t4(t, bk=bk):
                        t.wait_ge(sems["ve"], sems.val("ve"))
                        t.matmul(bk[0:128, 48:64], ones1b[:, :], rowacb[0:1, :],
                                 start=True, stop=True).then_inc(sems["mm"], 1)
                    b.tensor(_t4)
                    sems.c["mm"] += 1

                    def _e4(v, j=j, bk=bk):
                        v.wait_ge(sems["mm"], sems.val("mm"))
                        v.tensor_copy(aC_col[:, j * 16:(j + 1) * 16],
                                      bk[0:128, 48:64]).then_inc(sems["ve"], 1)
                    b.vector(_e4)
                    sems.c["ve"] += 1

            for j in range(NCH):
                emit_p3_chunk(j)

            with nc.Block(no_gpsimd_drain=True) as b:
                @b.gpsimd
                def _(g):
                    g.dma_start(out=cc2_in.ap(), in_=A1(Tso, 2048)).then_inc(*sems.inc("xd", 16))
                    g.wait_ge(sems["xd"], sems.val("xd"))
                    g.collective_compute("AllGather", ALU.bypass, replica_groups=RG,
                                         ins=[cc2_in.ap().opt()],
                                         outs=[cc2_out.ap().opt()]).then_inc(*sems.inc("cc"))
                    g.wait_ge(sems["cc"], sems.val("cc"))
                    g.dma_start(out=A1(Tso, 2048),
                                in_=cc2_out.ap()[0:128, :]).then_inc(*sems.inc("xd", 16))
                    g.wait_ge(sems["xd"], sems.val("xd"))

                @b.vector
                def _(v):
                    v.wait_ge(sems["xd"], sems.val("xd"))
                    v.tensor_scalar(A1(Tso, 2048), A1(Tso, 2048), halfm_s[:, 0:1],
                                    None, ALU.mult).then_inc(*sems.inc("ve"))

            with nc.Block(no_gpsimd_drain=True) as b:
                for j in range(NCH):
                    for g in range(4):
                        h0 = 4 * g
                        bank = PS[4 + ((j * 4 + g) % 2)]

                        def _pb(t, j=j, h0=h0, bank=bank):
                            if sems.val("mm") > 0:
                                t.wait_ge(sems["ve"], sems.val("ve"))
                            for q in range(4):
                                h = h0 + q
                                ins = t.matmul(bank[:, q * 128:(q + 1) * 128],
                                               A1(Info + j * 2048 + h * 128, 128),
                                               A1(Tso + h * 128, 128),
                                               start=True, stop=True)
                            ins.then_inc(sems["mm"], 1)
                        b.tensor(_pb)
                        sems.c["mm"] += 1

                        def _pbe(v, j=j, h0=h0, bank=bank):
                            v.wait_ge(sems["mm"], sems.val("mm"))
                            for q in range(4):
                                h = h0 + q
                                ins = v.scalar_tensor_tensor(
                                    qslice(h, j), bank[:, q * 128:(q + 1) * 128],
                                    a_col[:, j * 16 + h:j * 16 + h + 1],
                                    qslice(h, j), ALU.mult, ALU.add)
                            ins.then_inc(sems["ve"], 1)
                        b.vector(_pbe)
                        sems.c["ve"] += 1

        def emit_p4():
            xnf = A1(0, NF * T)
            wslg = (A1(16384, 2048), A1(18432, 2048))
            GOFF = 20480

            for tc0 in range(0, T, TC):
                emit_ln_chunk_to(tc0, xnf, T)

            def evac_g(e, mi, ps, ci):
                return e.activation(A1(GOFF + mi * T + ci * TC, TC), ps, AF.Copy)

            stream_matmul(gT, HID, NF, xnf, TC, wslg, 1, 2, evac_g,
                          rstride=T, roffs=(0, TC))

            with nc.Block(no_gpsimd_drain=True) as b:
                @b.scalar
                def _(e):
                    for f in range(NF):
                        ins = e.activation(A1(GOFF + f * T, T), A1(GOFF + f * T, T),
                                           AF.Silu)
                    ins.then_inc(*sems.inc("ev"))

            dump = A1(36864, 128)
            rms2_s = rms_s
            rmsB = A1(36992, 32, F32)
            LSP = [("mm", "ev", "ve"), ("mmB", "evB", "veB")]

            def p4_steps(j):
                L = j % 2
                smm, sev, sve = LSP[L]
                rms = rms_s[:, :] if L == 0 else rmsB
                bk = (PS[4], PS[5]) if L == 0 else (PS[6], PS[7])

                def squares(e):
                    for h in range(H):
                        ins = e.activation(dump, qslice(h, j), AF.Square,
                                           accum_out=rms[:, h:h + 1])
                    ins.then_inc(sems[sev], 1)
                    sems.c[sev] += 1

                def lnexp(e):
                    e.activation(rms, rms, AF.Ln, scale=1.0 / 128.0, bias=EPS_RMS)
                    e.activation(rms, rms, AF.Exp, scale=-0.5).then_inc(sems[sev], 1)
                    sems.c[sev] += 1

                def stt(v):
                    v.wait_ge(sems[sev], sems.val(sev))
                    for h in range(H):
                        ins = v.scalar_tensor_tensor(qslice(h, j), qslice(h, j),
                                                     rms[:, h:h + 1], onw_s[:, :],
                                                     ALU.mult, ALU.mult)
                    ins.then_inc(sems[sve], 1)
                    sems.c[sve] += 1

                steps = [("scalar", squares), ("scalar", lnexp), ("vector", stt)]
                for g in range(4):
                    h0 = g * 4
                    bank = bk[g % 2]

                    def tr(t, h0=h0, bank=bank):
                        t.wait_ge(sems[sve], sems.val(sve))
                        for q in range(4):
                            ins = t.matmul(bank[:, q * 128:(q + 1) * 128],
                                           qslice(h0 + q, j), identb[:, :],
                                           start=True, stop=True)
                        ins.then_inc(sems[smm], 1)
                        sems.c[smm] += 1
                    steps.append(("tensor", tr))

                    def tre(v, h0=h0, bank=bank):
                        v.wait_ge(sems[smm], sems.val(smm))
                        ov = a2ap(h0 * T + j * C, [[T, 4], [1, 128]])
                        gv = a1ap(GOFF + h0 * T + j * C, [[T, 4], [1, 128]])
                        v.tensor_tensor(ov, bank[:, :].rearrange(
                            "p (h c) -> p h c", c=128), gv,
                            ALU.mult).then_inc(sems[sve], 1)
                        sems.c[sve] += 1
                    steps.append(("vector", tre))
                return steps

            with nc.Block(no_gpsimd_drain=True) as b:
                for j0 in range(0, NCH, 2):
                    st0 = p4_steps(j0)
                    st1 = p4_steps(j0 + 1)
                    for si in range(len(st0)):
                        eng, fn = st0[si]
                        getattr(b, eng)(fn)
                        eng, fn = st1[si]
                        getattr(b, eng)(fn)

            wslo = (A1(16384, 2048), A1(18432, 2048))

            def evac_ow(v, mi, ps, ci):
                return v.tensor_tensor(resid[:, mi * T + ci * TC:mi * T + ci * TC + TC],
                                       resid[:, mi * T + ci * TC:mi * T + ci * TC + TC],
                                       ps, ALU.add)

            stream_matmul(oTw, HID, NF, arena2[:, 0:16 * T], TC, wslo, 1, 2,
                          evac_ow, rstride=T, roffs=(0, TC), evac_engine="vector")

        with nc.named_scope("P1ffn"):
            emit_ffn_sublayer(True)

        if _PHASES >= 2:
            with nc.named_scope("P2proj"):
                emit_p2()
        if _PHASES >= 3:
            with nc.named_scope("P3delta"):
                emit_p3()
        if _PHASES >= 4:
            with nc.named_scope("P4out"):
                emit_p4()
        if _PHASES >= 5:
            with nc.named_scope("P5ffn"):
                emit_ffn_sublayer(False)

        if _DBG_QKVC:
            dbg = nc.dram_tensor("dbg", [QKVD, T], F32, kind="ExternalOutput").ap()
            with nc.Block(no_gpsimd_drain=True) as b:
                @b.gpsimd
                def _(g):
                    g.dma_start(out=dbg.rearrange("(q p) t -> p q t", p=128),
                                in_=arena2[:, 0:NQT * T].rearrange("p (q t) -> p q t", q=NQT)).then_inc(*sems.inc("st", 16))
                    g.wait_ge(sems["st"], sems.val("st"))
        if _DBG_STATE:
            dbg2 = nc.dram_tensor("dbg2", [128, H * DK], F32, kind="ExternalOutput").ap()
            dbg3 = nc.dram_tensor("dbg3", [128, H * DK], F32, kind="ExternalOutput").ap()
            with nc.Block(no_gpsimd_drain=True) as b:
                @b.gpsimd
                def _(g):
                    g.dma_start(out=dbg2, in_=A1(Tso, 2048)).then_inc(*sems.inc("st", 16))
                    g.dma_start(out=dbg3, in_=A1(Pho, 2048)).then_inc(*sems.inc("st", 16))
                    g.wait_ge(sems["st"], sems.val("st"))

        with nc.Block(no_gpsimd_drain=True) as b:
            @b.sync
            def _(e):
                e.dma_start(out=out_d.rearrange("(f p) t -> p f t", p=128),
                            in_=resid[:, :].rearrange("p (f t) -> p f t", f=NF)).then_inc(*sems.inc("st", 16))
                e.wait_ge(sems["st"], sems.val("st"))

    return nc


CONSTS = [1.0, EPS_LN, EPS_L2, -0.5, 0.5, float(-0.5 * np.log(128.0)),
          EPS_RMS, 1.0 / 128.0, -1.0, -1.0 / 16.0, 128.0, 0.0]


def _host_inputs(inputs):
    x = np.ascontiguousarray(np.asarray(inputs["L0_attn_hidden_states"]))
    B, Sq, _ = x.shape
    xt = x.reshape(B * Sq, HID)
    def _tile_w(W, mg):
        WT = np.asarray(W).T
        kdim, M = WT.shape
        kt, sc = kdim // 128, mg * 128
        nst = M // sc
        arr = WT.reshape(kt, 128, nst, sc).transpose(2, 1, 0, 3)
        return np.ascontiguousarray(arr).reshape(nst, 128, kt * sc)

    base = {
        "qkvT": _tile_w(inputs["L0_attn_qkv_w"], 1),
        "gT": _tile_w(inputs["L0_attn_g_w"], 1),
        "bT": np.ascontiguousarray(np.asarray(inputs["L0_attn_b_w"]).T),
        "gkT": np.ascontiguousarray(np.asarray(inputs["L0_attn_gk_w"]).T),
        "oT": _tile_w(inputs["L0_attn_o_w"], 1),
        "WgT": np.ascontiguousarray(np.asarray(inputs["L0_ffn_Wg"]).T),
        "W0aT": _tile_w(inputs["L0_ffn_W0a"], 2),
        "W0bT": _tile_w(inputs["L0_ffn_W0b"], 1),
        "W1aT": _tile_w(inputs["L0_ffn_W1a"], 2),
        "W1bT": _tile_w(inputs["L0_ffn_W1b"], 1),
        "convw": np.ascontiguousarray(np.asarray(inputs["L0_attn_conv_w"])),
        "b0a_c": np.ascontiguousarray(np.asarray(inputs["L0_ffn_b0a"]).reshape(NFF, 128).T),
        "b1a_c": np.ascontiguousarray(np.asarray(inputs["L0_ffn_b1a"]).reshape(NFF, 128).T),
        "b0b_r": np.ascontiguousarray(np.asarray(inputs["L0_ffn_b0b"]).reshape(1, HID)),
        "b1b_r": np.ascontiguousarray(np.asarray(inputs["L0_ffn_b1b"]).reshape(1, HID)),
        "onw_b": np.ascontiguousarray(
            np.broadcast_to(np.asarray(inputs["L0_attn_out_norm_w"]), (128, 128))),
    }
    in_maps = []
    for c in range(8):
        bidx, half = c // 2, c % 2
        t0 = bidx * Sq + half * T
        m = dict(base)
        m["x"] = np.ascontiguousarray(xt[t0:t0 + T, :].T)
        m["halfm"] = np.full((128, 1), float(half), np.float32)
        in_maps.append(m)
    return in_maps


def _run(inputs):
    nc = build()
    in_maps = _host_inputs(inputs)
    return run_bass_kernel_spmd(nc, in_maps, core_ids=list(range(8)))


def kernel(**inputs):
    res = _run(inputs)
    B, Sq = 4, 2048
    full = np.empty((B, Sq, HID), np.float32)
    for c in range(8):
        bidx, half = c // 2, c % 2
        full[bidx, half * T:(half + 1) * T, :] = res.results[c]["out"].T
    return full
